# version 01
# baseline (speedup 1.0000x reference)
import numpy as np
import ml_dtypes
from contextlib import ExitStack
import concourse.bass as bass
import concourse.mybir as mybir
from concourse.bass_utils import run_bass_kernel_spmd

F32 = mybir.dt.float32
BF16 = mybir.dt.bfloat16
AF = mybir.ActivationFunctionType
ALU = mybir.AluOpType
AX = mybir.AxisListType
BF = ml_dtypes.bfloat16

D = 2048
NC = 8
TPC = 2048
LN_EPS = 1e-5
F32_EPS = 1.1920929e-07
RMAGIC = 12582912.0


class Prog:

    ENGS = ("sync", "tensor", "vector", "scalar", "gpsimd")

    def __init__(self, name="prog"):
        self.nc = bass.Bass("TRN2", target_bir_lowering=False)
        self.ops = {e: [] for e in self.ENGS}
        self.sem_val = {}
        self.sem_names = []
        for e in ("tensor", "vector", "scalar", "gpsimd"):
            self._new_sem("_" + e)
        self._consts = {}
        self._scoped = []

    def _new_sem(self, name):
        assert name not in self.sem_val
        self.sem_val[name] = 0
        self.sem_names.append(name)
        return name

    def dma_sem(self, name):
        if name not in self.sem_val:
            self._new_sem(name)
        return name

    def op(self, eng, fn, waits=(), inc=1, sync=True, tag=""):
        wl = [(s, v) for t in waits if t is not None for (s, v) in [t] if s is not None and v > 0]
        sem = "_" + eng if eng != "sync" else None
        if sync and eng in ("vector", "scalar", "gpsimd") and self.sem_val[sem] > 0:
            w = (sem, self.sem_val[sem])
            if w not in wl:
                wl.append(w)
        tok = None
        if inc and sem is not None:
            self.sem_val[sem] += inc
            tok = (sem, self.sem_val[sem])
        self.ops[eng].append((tuple(wl), fn, sem if inc else None, inc, tag))
        return tok

    def dma(self, queue, out, in_, sem, waits=(), tag="", **kw):
        self.dma_sem(sem)
        wl = [(s, v) for t in waits if t is not None for (s, v) in [t] if s is not None and v > 0]
        self.sem_val[sem] += 16
        tok = (sem, self.sem_val[sem])
        def fn(h, out=out, in_=in_, kw=kw):
            return h.dma_start(out=out, in_=in_, **kw)
        self.ops[queue].append((tuple(wl), fn, sem, 16, tag))
        return tok

    def wait(self, eng, waits, tag=""):
        wl = [(s, v) for t in waits if t is not None for (s, v) in [t] if s is not None and v > 0]
        if wl:
            self.ops[eng].append((tuple(wl), None, None, 0, tag))

    def tok(self, eng):
        return ("_" + eng, self.sem_val["_" + eng])

    def sbuf(self, name, shape, dt):
        return self.nc.alloc_sbuf_tensor(name, shape, dt)

    def sbuf_scoped(self, name, shape, dt):
        g = self.nc.sbuf_tensor(name, shape, dt)
        t = g.__enter__()
        self._scoped.append((name, g))
        return t

    def sbuf_unscope(self, name):
        assert self._scoped and self._scoped[-1][0] == name
        _, g = self._scoped.pop()
        g.__exit__(None, None, None)

    def psum(self, name, shape, dt=F32):
        return self.nc.alloc_psum_tensor(name, shape, dt)

    def const(self, name, arr):
        if name not in self._consts:
            self._consts[name] = self.nc.inline_tensor(np.ascontiguousarray(arr), name=name)
        return self._consts[name]

    def inp(self, name, shape, dt=F32):
        return self.nc.dram_tensor(name, list(shape), dt, kind="ExternalInput").ap()

    def out(self, name, shape, dt=F32):
        return self.nc.dram_tensor(name, list(shape), dt, kind="ExternalOutput").ap()

    def dram(self, name, shape, dt=F32):
        return self.nc.dram_tensor(name, list(shape), dt, kind="Internal").ap()

    def build(self):
        nc = self.nc
        with ExitStack() as stk:
            sems = {n: stk.enter_context(nc.semaphore(n)) for n in self.sem_names}
            block = stk.enter_context(nc.Block())
            def mk(e):
                def run(h):
                    for waits, fn, sem, inc, tag in self.ops[e]:
                        for (s, v) in waits:
                            h.wait_ge(sems[s], v)
                        if fn is not None:
                            inst = fn(h)
                            if sem is not None and inc:
                                inst.then_inc(sems[sem], inc)
                return run
            for e in self.ENGS:
                if self.ops[e]:
                    getattr(block, e)(mk(e))
        return nc


class _NS:
    pass


CONST_VALS = (0.0, 1.0, LN_EPS, F32_EPS, 1e-6, 1e-7, 1e-5)


def emit_common_consts(p, extra_vals=()):
    ns = _NS()
    vals = []
    for v in tuple(CONST_VALS) + tuple(extra_vals):
        if v not in vals:
            vals.append(v)
    c0 = p.const("c_vals", np.tile(np.array(vals, np.float32), (128, 1)))
    c1 = p.const("c_ones_f32", np.ones((128, 1), np.float32))
    c2 = p.const("c_ones_bf", np.ones((128, 1), BF))
    c3 = p.const("c_ones_row", np.ones((1, 128), np.float32))
    ns.cvals = p.sbuf("cvals", [128, len(vals)], F32)
    ns.ones_f32 = p.sbuf("ones_f32", [128, 1], F32)
    ns.ones_bf = p.sbuf("ones_bf", [128, 1], BF16)
    ns.ones_row = p.sbuf("ones_row", [1, 128], F32)
    for i, v in enumerate(vals):
        p.nc.const_aps.register(F32, float(v), ns.cvals[:, i:i + 1])
    sem = p.dma_sem("const_ld")
    p.dma("sync", ns.cvals[:], c0.ap(), sem)
    p.dma("sync", ns.ones_f32[:], c1.ap(), sem)
    p.dma("sync", ns.ones_bf[:], c2.ap(), sem)
    t3 = p.dma("sync", ns.ones_row[:], c3.ap(), sem)
    ns.ready = [t3]
    return ns


def emit_bcast_row(p, ns, row_ap, out_sb, ps_bank, wide, waits=()):
    last = None
    for j in range((wide + 511) // 512):
        sl = slice(j * 512, min((j + 1) * 512, wide))
        last = p.op("tensor", lambda h, sl=sl: h.matmul(
            ps_bank[:, sl], ns.ones_row[:], row_ap[:, sl], start=True, stop=True),
            waits=list(waits), inc=(1 if sl.stop == wide else 0))
    ev = p.op("vector", lambda h: h.tensor_copy(out_sb[:, 0:wide], ps_bank[:, 0:wide]), waits=[last])
    return ev


def emit_ln_rows(p, sums_ps, sumsq_ps, mu_row, rstd_row, tmp_row, nd, eps, wide, waits=()):
    c1 = p.op("vector", lambda h: h.tensor_scalar_mul(mu_row[:, 0:wide], sums_ps[:, 0:wide], 1.0 / nd),
              waits=list(waits))
    c2 = p.op("vector", lambda h: h.tensor_scalar_mul(rstd_row[:, 0:wide], sumsq_ps[:, 0:wide], 1.0 / nd))
    c3 = p.op("vector", lambda h: h.tensor_tensor(tmp_row[:, 0:wide], mu_row[:, 0:wide], mu_row[:, 0:wide], ALU.mult))
    c4 = p.op("vector", lambda h: h.tensor_tensor(rstd_row[:, 0:wide], rstd_row[:, 0:wide], tmp_row[:, 0:wide], ALU.subtract))
    s1 = p.op("scalar", lambda h: h.activation(rstd_row[:, 0:wide], rstd_row[:, 0:wide], AF.Ln, bias=float(eps)),
              waits=[c4])
    s2 = p.op("scalar", lambda h: h.activation(rstd_row[:, 0:wide], rstd_row[:, 0:wide], AF.Exp, scale=-0.5))
    return c2, s2, c1



def build_prep(qrows, crows):
    p = Prog("prep")
    W = 2048
    G = W // 64
    q_in = p.inp("q_in", [qrows, W])
    c_in = p.inp("c_in", [crows, W])
    q_out = p.out("q_out", [qrows, W], BF16)
    c_out = p.out("c_out", [crows, W], BF16)

    ld = p.dma_sem("ld")
    st = p.dma_sem("st")

    t32 = p.sbuf("t32", [128, W], F32)
    tb = [p.sbuf(f"tb{i}", [128, W], BF16) for i in range(2)]
    tb32 = p.sbuf("tb32", [128, W], F32)
    sc = p.sbuf("sc", [128, G], F32)
    scb = p.sbuf("scb", [128, G], BF16)
    rs = p.sbuf("rs", [128, G], F32)
    qv = p.sbuf("qv", [128, W], F32)
    hat = [p.sbuf(f"hat{i}", [128, W], BF16) for i in range(2)]
    last_store = [None, None]
    prev_read = None
    nq = qrows // 128
    for i in range(nq):
        par = i % 2
        tl = p.dma("sync", t32[:], q_in[i * 128:(i + 1) * 128, :], ld,
                   waits=[prev_read] if prev_read else [])
        v1 = p.op("vector", lambda h, par=par: h.tensor_copy(tb[par][:], t32[:]),
                  waits=[tl] + ([last_store[par]] if last_store[par] else []))
        prev_read = v1
        p.op("vector", lambda h, par=par: h.tensor_copy(tb32[:], tb[par][:]))
        p.op("vector", lambda h: h.tensor_reduce(
            sc[:], tb32[:].rearrange("p (g s) -> p g s", s=64), AX.X, ALU.add,
            apply_absolute_value=True))
        p.op("vector", lambda h: h.tensor_scalar(sc[:], sc[:], 1.0 / 64, 1e-8, ALU.mult, ALU.max))
        p.op("vector", lambda h: h.tensor_copy(scb[:], sc[:]))
        p.op("vector", lambda h: h.tensor_copy(sc[:], scb[:]))
        p.op("vector", lambda h: h.reciprocal(rs[:], sc[:]))
        p.op("vector", lambda h: h.tensor_tensor(
            qv[:].rearrange("p (g s) -> p g s", s=64),
            tb32[:].rearrange("p (g s) -> p g s", s=64),
            rs[:].unsqueeze(2).broadcast_to([128, G, 64]), ALU.mult))
        p.op("vector", lambda h: h.tensor_scalar(qv[:], qv[:], RMAGIC, RMAGIC, ALU.add, ALU.subtract))
        p.op("vector", lambda h: h.tensor_scalar(qv[:], qv[:], -1.0, 1.0, ALU.max, ALU.min))
        v11 = p.op("vector", lambda h, par=par: h.tensor_tensor(
            hat[par][:].rearrange("p (g s) -> p g s", s=64),
            qv[:].rearrange("p (g s) -> p g s", s=64),
            sc[:].unsqueeze(2).broadcast_to([128, G, 64]), ALU.mult))
        last_store[par] = p.dma("gpsimd", q_out[i * 128:(i + 1) * 128, :], hat[par][:], st, waits=[v11])

    c32 = [p.sbuf(f"c32{i}", [128, W], F32) for i in range(2)]
    cb = [p.sbuf(f"cb{i}", [128, W], BF16) for i in range(2)]
    lcast = [None, None]
    lcstore = [None, None]
    ncst = (crows + 127) // 128
    for i in range(ncst):
        par = i % 2
        r0, r1 = i * 128, min((i + 1) * 128, crows)
        nr = r1 - r0
        tl = p.dma("sync", c32[par][0:nr, :], c_in[r0:r1, :], ld,
                   waits=[lcast[par]] if lcast[par] else [])
        cv = p.op("vector", lambda h, par=par, nr=nr: h.tensor_copy(cb[par][0:nr, :], c32[par][0:nr, :]),
                  waits=[tl] + ([lcstore[par]] if lcstore[par] else []))
        lcast[par] = cv
        lcstore[par] = p.dma("gpsimd", c_out[r0:r1, :], cb[par][0:nr, :], st, waits=[cv])

    p.wait("gpsimd", [("st", p.sem_val["st"])])
    return p.build()



def build_ffn(emit_out_ln=True):
    p = Prog("ffn")
    ns = emit_common_consts(p)
    x_in = p.inp("x", [16, 128, TPC])
    wguT = p.inp("wguT", [2048, 16384], BF16)
    wpT = p.inp("wpT", [8192, 2048], BF16)
    x_out = p.out("xo", [16, 128, TPC])
    lnx_out = p.out("lnxo", [16, 128, TPC], BF16) if emit_out_ln else None
    h_dram = p.dram("h_scr", [64, 128, TPC], BF16)

    ld = p.dma_sem("x_ld"); wld = p.dma_sem("w_ld"); hst = p.dma_sem("h_st")
    hld = p.dma_sem("h_ld"); xst = p.dma_sem("x_st")

    PA = p.psum("PA", [128, 2048], F32)
    PB = p.psum("PB", [128, 2048], F32)

    mu_row = p.sbuf("mu_row", [1, TPC], F32)
    rstd_row = p.sbuf("rstd_row", [1, TPC], F32)
    tmp_row = p.sbuf("tmp_row", [1, TPC], F32)
    sg = p.sbuf("sg", [128, TPC], F32)
    hbuf = [p.sbuf(f"hb{i}", [128, TPC], BF16) for i in range(2)]

    xt = [p.sbuf_scoped(f"xt{i}", [128, TPC], F32) for i in range(2)]
    sqs = p.sbuf_scoped("sqs", [128, TPC], BF16)
    mu_b = p.sbuf_scoped("mu_b", [128, TPC], F32)
    rstd_b = p.sbuf_scoped("rstd_b", [128, TPC], F32)
    tmp_ln = p.sbuf_scoped("tmp_ln", [128, TPC], F32)
    wg = [p.sbuf_scoped(f"wg{i}", [128, 16 * 128], BF16) for i in range(2)]
    wu = [p.sbuf_scoped(f"wu{i}", [128, 16 * 128], BF16) for i in range(2)]
    lnx = p.sbuf_scoped("lnx", [128, 16 * TPC], BF16)
    sums = PA[0:1, :]
    sumsq = PB[0:1, :]
    last_stat_mm = None
    for k in range(16):
        par = k % 2
        tl = p.dma("sync", xt[par][:], x_in[k], ld,
                   waits=[last_stat_mm] if k >= 2 else [])
        s1 = p.op("scalar", lambda h, par=par: h.activation(sqs[:], xt[par][:], AF.Square),
                  waits=[tl] + ([last_stat_mm] if last_stat_mm else []))
        for j in range(4):
            sl = slice(j * 512, (j + 1) * 512)
            p.op("tensor", lambda h, sl=sl, par=par, k=k: h.matmul(
                sums[:, sl], ns.ones_f32[:], xt[par][:, sl], start=(k == 0), stop=(k == 15)),
                waits=([tl] + ns.ready) if j == 0 else (), inc=0)
        for j in range(4):
            sl = slice(j * 512, (j + 1) * 512)
            last_stat_mm = p.op("tensor", lambda h, sl=sl, k=k: h.matmul(
                sumsq[:, sl], ns.ones_bf[:], sqs[:, sl], start=(k == 0), stop=(k == 15)),
                waits=[s1] if j == 0 else (), inc=(1 if j == 3 else 0))
    c2, s2, c1 = emit_ln_rows(p, sums, sumsq, mu_row, rstd_row, tmp_row, D, LN_EPS, TPC,
                              waits=[last_stat_mm])
    ev_mu = emit_bcast_row(p, ns, mu_row, mu_b, PB, TPC, waits=[c2, c1])
    ev_rs = emit_bcast_row(p, ns, rstd_row, rstd_b, PB, TPC, waits=[s2, ev_mu])
    lnx_ready = None
    for k in range(16):
        par = k % 2
        tl = p.dma("sync", xt[par][:], x_in[k], ld,
                   waits=[last_stat_mm] if k < 2 else [a1])
        a1 = p.op("vector", lambda h, par=par: h.tensor_tensor(tmp_ln[:], xt[par][:], mu_b[:], ALU.subtract),
                  waits=[tl, ev_mu, ev_rs])
        lnx_ready = p.op("vector", lambda h, k=k: h.tensor_tensor(
            lnx[:, k * TPC:(k + 1) * TPC], tmp_ln[:], rstd_b[:], ALU.mult))

    gate_ps, up_ps = PA, PB
    wguT_v = wguT.rearrange("(k p) r -> p k r", p=128)
    last_w_use = [None, None]
    last_h_store = [None, None]
    last_silu = None
    last_hmul = None
    for j in range(64):
        par = j % 2
        wt = [last_w_use[par]] if last_w_use[par] else []
        tg = p.dma("sync", wg[par][:].rearrange("p (k r) -> p k r", k=16),
                   wguT_v[:, :, j * 128:(j + 1) * 128], wld, waits=wt)
        tu = p.dma("sync", wu[par][:].rearrange("p (k r) -> p k r", k=16),
                   wguT_v[:, :, 8192 + j * 128:8192 + (j + 1) * 128], wld, waits=wt)
        gw = [tg, lnx_ready] + ([last_silu] if last_silu else [])
        g_mm = None
        for k in range(16):
            for t in range(4):
                sl = slice(t * 512, (t + 1) * 512)
                g_mm = p.op("tensor", lambda h, k=k, sl=sl, par=par: h.matmul(
                    gate_ps[:, sl], wg[par][:, k * 128:(k + 1) * 128],
                    lnx[:, k * TPC:(k + 1) * TPC][:, sl], start=(k == 0), stop=(k == 15)),
                    waits=gw if (k == 0 and t == 0) else (), inc=(1 if (k == 15 and t == 3) else 0))
        uw = [tu] + ([last_hmul] if last_hmul else [])
        u_mm = None
        for k in range(16):
            for t in range(4):
                sl = slice(t * 512, (t + 1) * 512)
                u_mm = p.op("tensor", lambda h, k=k, sl=sl, par=par: h.matmul(
                    up_ps[:, sl], wu[par][:, k * 128:(k + 1) * 128],
                    lnx[:, k * TPC:(k + 1) * TPC][:, sl], start=(k == 0), stop=(k == 15)),
                    waits=uw if (k == 0 and t == 0) else (), inc=(1 if (k == 15 and t == 3) else 0))
        last_w_use[par] = u_mm
        last_silu = p.op("scalar", lambda h: h.activation(sg[:], gate_ps[:], AF.Silu), waits=[g_mm])
        last_hmul = p.op("vector", lambda h, par=par: h.tensor_tensor(hbuf[par][:], sg[:], up_ps[:], ALU.mult),
                         waits=[last_silu, u_mm] + ([last_h_store[par]] if last_h_store[par] else []))
        last_h_store[par] = p.dma("gpsimd", h_dram[j], hbuf[par][:], hst, waits=[last_hmul])
    for nm in ("lnx", "wu1", "wu0", "wg1", "wg0", "tmp_ln", "rstd_b", "mu_b", "sqs", "xt1", "xt0"):
        p.sbuf_unscope(nm)
    all_h_stored = [last_h_store[0], last_h_store[1]]
    pa_free = last_silu
    pb_free = last_hmul

    hslab = p.sbuf_scoped("hslab", [128, 64 * 512], BF16)
    wp_sb = [p.sbuf_scoped(f"wp_sb{i}", [128, 64 * 128], BF16) for i in range(2)]
    xn = p.sbuf_scoped("xn", [128, 16 * 512], F32)
    hsq = p.sbuf_scoped("hsq", [128, 512], BF16)
    xsq = p.sbuf_scoped("xsq", [128, 512], BF16)
    xo_t = p.sbuf_scoped("xo_t", [128, 512], F32)
    tmp_o = p.sbuf_scoped("tmp_o", [128, 512], F32)
    lnxo_t = p.sbuf_scoped("lnxo_t", [128, 512], BF16)
    rstdh_b = p.sbuf_scoped("rstdh_b", [128, 512], F32)
    omu_b = p.sbuf_scoped("omu_b", [128, 512], F32)
    orstd_b = p.sbuf_scoped("orstd_b", [128, 512], F32)

    ssq_ps = PA[0:1, 0:512]
    bc_ps = PA[:, 512:1024]
    osums_ps = PA[0:1, 1024:1536]
    osumsq_ps = PA[0:1, 1536:2048]
    down_banks = [PB[:, j * 512:(j + 1) * 512] for j in range(4)]

    wpT_v = wpT.rearrange("(hh p) d -> p hh d", p=128)
    last_wp_use = [None, None]
    last_xo_read = None
    prev_slab_done = None
    prev_r1 = None
    prev_of = None
    last_bcast_ev = ev_rs
    down_evac_by_bank = [None] * 4

    for s in range(4):
        tsl = slice(s * 512, (s + 1) * 512)
        h_toks = []
        for j in range(64):
            w = []
            if s == 0:
                w = all_h_stored
            elif j < 2:
                w = [prev_slab_done, prev_sq_last]
            h_toks.append(p.dma("sync", hslab[:, j * 512:(j + 1) * 512], h_dram[j][:, tsl], hld, waits=w))
        last_ssq_mm = None
        sqt = None
        for j in range(64):
            sqt = p.op("scalar", lambda h, j=j: h.activation(hsq[:], hslab[:, j * 512:(j + 1) * 512], AF.Square),
                       waits=[h_toks[j]] + ([last_ssq_mm] if last_ssq_mm else []))
            last_ssq_mm = p.op("tensor", lambda h, j=j: h.matmul(
                ssq_ps[:], ns.ones_bf[:], hsq[:], start=(j == 0), stop=(j == 63)),
                waits=[sqt] + ([pa_free, prev_r1] if j == 0 else []), inc=1)
        prev_sq_last = sqt
        r1 = p.op("scalar", lambda h: h.activation(tmp_row[:, 0:512], ssq_ps[:], AF.Ln,
                                                   scale=1.0 / 8192, bias=F32_EPS),
                  waits=[last_ssq_mm])
        r2 = p.op("scalar", lambda h: h.activation(tmp_row[:, 0:512], tmp_row[:, 0:512], AF.Exp, scale=-0.5))
        r3 = p.op("vector", lambda h: h.tensor_scalar_mul(tmp_row[:, 0:512], tmp_row[:, 0:512], 0.5), waits=[r2])
        prev_r1 = r1
        ev_rh = emit_bcast_row(p, ns, tmp_row, rstdh_b, bc_ps, 512,
                               waits=[r3, last_bcast_ev] + ([pa_free] if s == 0 else []))
        last_bcast_ev = ev_rh
        o_mm_toks = []
        for dout in range(16):
            par = dout % 2
            twp = p.dma("sync", wp_sb[par][:].rearrange("p (k r) -> p k r", k=64),
                        wpT_v[:, :, dout * 128:(dout + 1) * 128], wld,
                        waits=[last_wp_use[par]] if last_wp_use[par] else [])
            bank = down_banks[dout % 4]
            bw = [twp, h_toks[-1]]
            if s == 0 and dout < 4:
                bw.append(pb_free)
            if down_evac_by_bank[dout % 4] is not None:
                bw.append(down_evac_by_bank[dout % 4])
            mmt = None
            for j in range(64):
                mmt = p.op("tensor", lambda h, j=j, par=par, bank=bank: h.matmul(
                    bank[:], wp_sb[par][:, j * 128:(j + 1) * 128],
                    hslab[:, j * 512:(j + 1) * 512], start=(j == 0), stop=(j == 63)),
                    waits=bw if j == 0 else (), inc=(1 if j == 63 else 0))
            last_wp_use[par] = mmt
            txo = p.dma("sync", xo_t[:], x_in[dout][:, tsl], ld,
                        waits=[last_xo_read] if last_xo_read else [])
            m1 = p.op("vector", lambda h, bank=bank: h.tensor_tensor(tmp_o[:], bank[:], rstdh_b[:], ALU.mult),
                      waits=[mmt, ev_rh])
            down_evac_by_bank[dout % 4] = m1
            m2 = p.op("vector", lambda h, dout=dout: h.tensor_tensor(
                xn[:, dout * 512:(dout + 1) * 512], tmp_o[:], xo_t[:], ALU.add), waits=[txo])
            last_xo_read = m2
            p.dma("gpsimd", x_out[dout][:, tsl], xn[:, dout * 512:(dout + 1) * 512], xst, waits=[m2])
            if emit_out_ln:
                sq1 = p.op("scalar", lambda h, dout=dout: h.activation(
                    xsq[:], xn[:, dout * 512:(dout + 1) * 512], AF.Square),
                    waits=[m2] + (o_mm_toks[-1:] if o_mm_toks else []))
                p.op("tensor", lambda h, dout=dout: h.matmul(
                    osums_ps[:], ns.ones_f32[:], xn[:, dout * 512:(dout + 1) * 512],
                    start=(dout == 0), stop=(dout == 15)),
                    waits=([m2] + ([pa_free] if s == 0 else []) + ([prev_of] if prev_of else [])) , inc=0)
                t = p.op("tensor", lambda h, dout=dout: h.matmul(
                    osumsq_ps[:], ns.ones_bf[:], xsq[:],
                    start=(dout == 0), stop=(dout == 15)), waits=[sq1], inc=1)
                o_mm_toks.append(t)
        prev_slab_done = last_xo_read
        if emit_out_ln:
            oc2, os2, oc1 = emit_ln_rows(p, osums_ps, osumsq_ps, mu_row, rstd_row, tmp_row,
                                         D, LN_EPS, 512, waits=[o_mm_toks[-1]])
            prev_of = oc2
            ev_om = emit_bcast_row(p, ns, mu_row, omu_b, bc_ps, 512, waits=[oc1, oc2, last_bcast_ev])
            ev_or = emit_bcast_row(p, ns, rstd_row, orstd_b, bc_ps, 512, waits=[os2, ev_om])
            last_bcast_ev = ev_or
            ln_store = None
            for dout in range(16):
                a1 = p.op("vector", lambda h, dout=dout: h.tensor_tensor(
                    tmp_o[:], xn[:, dout * 512:(dout + 1) * 512], omu_b[:], ALU.subtract),
                    waits=[ev_om, ev_or] + ([ln_store] if ln_store else []))
                a2 = p.op("vector", lambda h: h.tensor_tensor(lnxo_t[:], tmp_o[:], orstd_b[:], ALU.mult))
                ln_store = p.dma("gpsimd", lnx_out[dout][:, tsl], lnxo_t[:], xst, waits=[a2])
            prev_slab_done = a2

    p.wait("gpsimd", [("x_st", p.sem_val["x_st"])])
    return p.build()



def _fm(x):
    return np.ascontiguousarray(np.transpose(x, (1, 0)).reshape(16, 128, -1))


def _fm_inv(xt):
    return np.ascontiguousarray(xt.reshape(2048, -1).T)


def run_prep(inp, _run):
    wgu = np.asarray(inp["L0_ffn_w_gate_up"])
    wp = np.asarray(inp["L0_ffn_w_proj"])
    l0w = np.concatenate([np.asarray(inp["L0_attn_W" + s]) for s in "qkvo"], 0)
    gatew = np.asarray(inp["L1_ffn_gate_w"])
    upw = np.asarray(inp["L1_ffn_up_w"])
    downw = np.asarray(inp["L1_ffn_down_w"])
    l1w = np.concatenate([np.asarray(inp["L1_attn_W_qkv"]), np.asarray(inp["L1_attn_W_o"])], 1)
    aux = np.concatenate([np.asarray(inp["L1_attn_W_lr"]), np.asarray(inp["L1_attn_W_m"]),
                          np.asarray(inp["L1_attn_W_s"]),
                          np.ascontiguousarray(np.asarray(inp["L1_ffn_router_w"]).T)], 1)
    q_slices, c_slices = [], []
    for c in range(NC):
        q = np.concatenate([wgu[c * 2048:(c + 1) * 2048],
                            wp[c * 256:(c + 1) * 256].reshape(1024, 2048)], 0)
        cc = np.concatenate([
            np.concatenate([l0w[m * 2048 + c * 256: m * 2048 + (c + 1) * 256] for m in range(4)], 0),
            gatew[c], upw[c], downw[c].reshape(1408, 2048),
            l1w[c * 256:(c + 1) * 256].reshape(1024, 2048),
            np.concatenate([aux[c * 256:(c + 1) * 256].reshape(-1),
                            np.zeros(2048 * 11 - 256 * 88, np.float32)]).reshape(11, 2048),
        ], 0)
        q_slices.append(np.ascontiguousarray(q))
        c_slices.append(np.ascontiguousarray(cc))
    nc_prep = build_prep(3072, 6283)
    res = _run(nc_prep, [{"q_in": q_slices[c], "c_in": c_slices[c]} for c in range(NC)],
               core_ids=list(range(NC)))
    R = res.results
    w = _NS()
    wgu_q = np.concatenate([R[c]["q_out"][0:2048] for c in range(NC)], 0)
    wp_q = np.concatenate([R[c]["q_out"][2048:3072].reshape(256, 8192) for c in range(NC)], 0)
    l0w_b = np.concatenate([np.concatenate([R[c]["c_out"][m * 256:(m + 1) * 256] for c in range(NC)], 0)
                            for m in range(4)], 0)
    w.gate_b = np.stack([R[c]["c_out"][1024:1024 + 1408] for c in range(NC)], 0)
    w.up_b = np.stack([R[c]["c_out"][2432:2432 + 1408] for c in range(NC)], 0)
    w.down_b = np.stack([R[c]["c_out"][3840:3840 + 1408].reshape(2048, 1408) for c in range(NC)], 0)
    l1w_b = np.concatenate([R[c]["c_out"][5248:5248 + 1024].reshape(256, 8192) for c in range(NC)], 0)
    aux_b = np.stack([R[c]["c_out"][6272:6283].reshape(-1)[0:256 * 88].reshape(256, 88) for c in range(NC)], 0)
    aux_b = np.concatenate(list(aux_b), 0)
    w.wguT = np.ascontiguousarray(wgu_q.T)
    w.wpT = np.ascontiguousarray(wp_q.T)
    w.wq_m = np.ascontiguousarray(l0w_b[0:2048].T)
    w.wk_m = np.ascontiguousarray(l0w_b[2048:4096].T)
    w.wv_m = np.ascontiguousarray(l0w_b[4096:6144].T)
    w.wo_m = np.ascontiguousarray(l0w_b[6144:8192].T)
    w.gateT = np.ascontiguousarray(np.transpose(w.gate_b, (0, 2, 1)))
    w.upT = np.ascontiguousarray(np.transpose(w.up_b, (0, 2, 1)))
    w.downT = np.ascontiguousarray(np.transpose(w.down_b, (0, 2, 1)))
    w.qkv_b = np.ascontiguousarray(l1w_b[:, 0:6144])
    w.wo1_b = np.ascontiguousarray(l1w_b[:, 6144:8192])
    w.aux_b = np.ascontiguousarray(aux_b[:, 0:80])
    w.routerT = np.ascontiguousarray(aux_b[:, 80:88])
    return w




def _chunk_major(v):
    return np.ascontiguousarray(np.asarray(v, np.float32).reshape(16, 128).T)


def run_l0_attn(inp, w, x_fm, lnx, _run):
    nc_c1 = _cached("l0attn", build_l0_attn_core)
    bq, bk, bv = (_chunk_major(inp["L0_attn_b" + s]) for s in "qkv")
    jg = np.float32(np.asarray(inp["L0_attn_justice_gate"])[0])
    og = np.float32(np.asarray(inp["L0_attn_order_gate"])[0])
    gates = np.array([[jg, og]], np.float32)
    in_maps = []
    for c in range(NC):
        b, half = c // 2, c % 2
        lnx_full = np.concatenate([lnx[2 * b], lnx[2 * b + 1]], axis=2)
        osl = slice(half * 1024, (half + 1) * 1024)
        in_maps.append({
            "lnx": lnx_full,
            "wq": np.ascontiguousarray(w.wq_m[:, osl]),
            "wk": np.ascontiguousarray(w.wk_m[:, osl]),
            "wv": np.ascontiguousarray(w.wv_m[:, osl]),
            "bqkv": np.ascontiguousarray(np.concatenate(
                [bq[:, half * 8:half * 8 + 8], bk[:, half * 8:half * 8 + 8],
                 bv[:, half * 8:half * 8 + 8]], axis=1)),
            "gates": gates,
        })
    res = _run(nc_c1, in_maps, core_ids=list(range(NC)))
    houts = [res.results[c]["hout"] for c in range(NC)]

    nc_c2 = _cached("projres", lambda: build_proj_residual(scale=1.0))
    bo = _chunk_major(inp["L0_attn_bo"])
    in_maps = []
    for c in range(NC):
        b, half = c // 2, c % 2
        tsl = slice(half * TPC, (half + 1) * TPC)
        hin = np.stack([houts[2 * b + hc // 8][hc % 8][:, tsl] for hc in range(16)], 0)
        in_maps.append({"hin": hin, "w": w.wo_m, "b": bo, "x": x_fm[c]})
    res = _run(nc_c2, in_maps, core_ids=list(range(NC)))
    x_fm = [res.results[c]["xo"] for c in range(NC)]
    lnx = [res.results[c]["lnxo"] for c in range(NC)]
    return x_fm, lnx


def kernel(**inp):
    _run = run_bass_kernel_spmd
    x0 = np.asarray(inp["L0_attn_hidden_states"])
    w = run_prep(inp, _run)

    x_fm = [_fm(x0[c // 2, (c % 2) * TPC:((c % 2) + 1) * TPC]) for c in range(NC)]

    nc_ffn = build_ffn()

    def run_ffn(x_fm):
        res = _run(nc_ffn, [{"x": x_fm[c], "wguT": w.wguT, "wpT": w.wpT} for c in range(NC)],
                   core_ids=list(range(NC)))
        return ([res.results[c]["xo"] for c in range(NC)],
                [res.results[c]["lnxo"] for c in range(NC)])

    x_fm, lnx = run_ffn(x_fm)
    x_fm, lnx = run_l0_attn(inp, w, x_fm, lnx, _run)
    x_fm, lnx = run_ffn(x_fm)
    x_fm, lnx = run_moe(inp, w, x_fm, lnx, _run)
    x_fm, lnx = run_l1_attn(inp, w, x_fm, lnx, _run)
    x_fm, lnx = run_moe(inp, w, x_fm, lnx, _run)

    out = np.zeros((4, 4096, D), np.float32)
    for c in range(NC):
        out[c // 2, (c % 2) * TPC:((c % 2) + 1) * TPC] = _fm_inv(x_fm[c])
    return out



def build_l0_attn_core(stage='full'):
    p = Prog("l0attn")
    ns = emit_common_consts(p)
    S = 4096
    ALPHA0 = float(1.0 / np.sqrt(128.0))
    lnx_in = p.inp("lnx", [16, 128, S], BF16)
    wq_c = p.inp("wq", [2048, 1024], BF16)
    wk_c = p.inp("wk", [2048, 1024], BF16)
    wv_c = p.inp("wv", [2048, 1024], BF16)
    bias_qkv = p.inp("bqkv", [128, 24], F32)
    gates = p.inp("gates", [1, 2], F32)
    hout = p.out("hout", [8, 128, S], BF16)

    ld = p.dma_sem("ld"); wldm = p.dma_sem("wldm"); ost = p.dma_sem("ost")

    PS = p.psum("PS", [128, 2048], F32)
    PO = p.psum("PO", [128, 512], F32)
    PD = p.psum("PD", [1, 512], F32)
    PN = p.psum("PN", [128, 512], F32)
    VPT = p.psum("VPT", [128, 1024], BF16)

    lnx = p.sbuf("lnxs", [128, 16 * S], BF16)
    wsb = [p.sbuf(f"wsb{i}", [128, 16 * 128], BF16) for i in range(2)]
    bsb = p.sbuf("bsb", [128, 24], F32)
    gsb = p.sbuf("gsb", [1, 2], F32)
    qt = p.sbuf("qt", [128, S], BF16)
    kt = p.sbuf("kt", [128, S], BF16)
    vt = p.sbuf("vt", [128, S], BF16)
    vtok = p.sbuf("vtok", [128, 32 * 128], BF16)
    sqb = p.sbuf("sqb", [128, S], BF16)
    ebuf = [p.sbuf(f"eb{i}", [128, 1024], BF16) for i in range(2)]
    iot = p.sbuf("iot", [128, 32], F32)
    fvec = p.sbuf("fvec", [128, 32], F32)
    fvecb = p.sbuf("fvecb", [128, 32], BF16)
    alphab = p.sbuf("alphab", [128, 1], F32)
    coefb = p.sbuf("coefb", [128, 1], F32)
    negCb = p.sbuf("negCb", [128, 1], F32)
    srow = p.sbuf("srow", [1, 16], F32)
    crow = p.sbuf("crow", [1, 4], F32)
    rrow = p.sbuf("rrow", [1, 512], F32)
    rb = p.sbuf("rb", [128, 512], F32)
    outtmp = p.sbuf("outtmp", [128, 512], BF16)
    idbf = p.sbuf("idbf", [128, 128], BF16)
    cid = p.const("c_id_bf", np.eye(128, dtype=BF))

    for k in range(16):
        t_lnx = p.dma("sync", lnx[:, k * S:(k + 1) * S], lnx_in[k], ld)
    t_b = p.dma("sync", bsb[:], bias_qkv, ld)
    t_g = p.dma("sync", gsb[:], gates, ld)
    t_id = p.dma("sync", idbf[:], cid.ap(), ld)
    lnx_ready = t_id

    gv = p.op("gpsimd", lambda h: h.iota(iot[:], [[128, 32]], base=0, channel_multiplier=1,
                                         allow_small_or_imprecise_dtypes=True))
    sj = p.op("scalar", lambda h: h.activation(srow[:, 0:2], gsb[:], AF.Sigmoid), waits=[t_g])
    a1 = p.op("vector", lambda h: h.tensor_scalar(srow[:, 2:3], srow[:, 0:1], -0.1 * ALPHA0, ALPHA0,
                                                  ALU.mult, ALU.add), waits=[sj])
    a2 = p.op("vector", lambda h: h.tensor_scalar(srow[:, 3:4], srow[:, 1:2], 0.05 / 4096.0, None,
                                                  ALU.mult))
    b1 = p.op("tensor", lambda h: h.matmul(PN[:, 0:1], ns.ones_row[:], srow[:, 2:3],
                                           start=True, stop=True), waits=[a1, a2] + ns.ready, inc=1)
    e1 = p.op("vector", lambda h: h.tensor_copy(alphab[:], PN[:, 0:1]), waits=[b1])
    b2 = p.op("tensor", lambda h: h.matmul(PN[:, 0:1], ns.ones_row[:], srow[:, 3:4],
                                           start=True, stop=True), waits=[e1], inc=1)
    e2 = p.op("vector", lambda h: h.tensor_copy(coefb[:], PN[:, 0:1]), waits=[b2])
    f1 = p.op("scalar", lambda h: h.activation(fvec[:], iot[:], AF.Exp, scale=coefb[:]),
              waits=[gv, e2])
    f2 = p.op("vector", lambda h: h.tensor_copy(fvecb[:], fvec[:]), waits=[f1])

    wq_v = wq_c.rearrange("(k p) o -> p k o", p=128)
    wk_v = wk_c.rearrange("(k p) o -> p k o", p=128)
    wv_v = wv_c.rearrange("(k p) o -> p k o", p=128)

    prev_norm_mm = b2
    prev_rb_ev = e2
    ps_free = [None, None]
    eb_free = [None, None]
    prev_store = None
    prev_proj_ev = f2
    prev_w_use = [None, None]
    for hh in range(8):
        oc = slice(hh * 128, (hh + 1) * 128)
        for m, (w_v, dst, boff) in enumerate(((wq_v, qt, 0), (wk_v, kt, 8), (wv_v, vt, 16))):
            par = m % 2
            tw = p.dma("sync", wsb[par][:].rearrange("p (k o) -> p k o", k=16),
                       w_v[:, :, oc], wldm, waits=[prev_w_use[par]])
            for tg in range(2):
                mm4 = [None] * 4
                for k in range(16):
                    for t in range(4):
                        tt = tg * 4 + t
                        mm4[t] = p.op("tensor", lambda h, k=k, tt=tt, par=par: h.matmul(
                            PS[:, (tt % 4) * 512:((tt % 4) + 1) * 512],
                            wsb[par][:, k * 128:(k + 1) * 128],
                            lnx[:, k * S:(k + 1) * S][:, tt * 512:(tt + 1) * 512],
                            start=(k == 0), stop=(k == 15)),
                            waits=([tw, lnx_ready, prev_proj_ev, ps_free[0], ps_free[1]] if (k == 0 and t == 0) else ()),
                            inc=(1 if (k == 15 and t == 3) else 0))
                for t in range(4):
                    tt = tg * 4 + t
                    prev_proj_ev = p.op("scalar", lambda h, m=m, tt=tt, t=t, boff=boff, hh=hh, dst=dst: h.activation(
                        dst[:, tt * 512:(tt + 1) * 512], PS[:, t * 512:(t + 1) * 512], AF.Identity,
                        bias=bsb[:, boff + hh:boff + hh + 1]), waits=[mm4[3], t_b])
            prev_w_use[par] = mm4[3]
        if stage == 'proj':
            for qb in range(8):
                qsl = slice(qb * 512, (qb + 1) * 512)
                cp = p.op("vector", lambda h, qsl=qsl: h.tensor_copy(outtmp[:], kt[:, qsl]),
                          waits=[prev_proj_ev] + ([prev_store] if prev_store else []))
                prev_store = p.dma("gpsimd", hout[hh][:, qsl], outtmp[:], ost, waits=[cp])
            continue
        vev = None
        for grp in range(4):
            tts = []
            for j8 in range(8):
                j = grp * 8 + j8
                tts.append(p.op("tensor", lambda h, j=j, j8=j8: h.transpose(
                    VPT[:, j8 * 128:(j8 + 1) * 128], vt[:, j * 128:(j + 1) * 128], idbf[:]),
                    waits=([prev_proj_ev, t_id, vev] if j8 == 0 else ()), inc=(1 if j8 == 7 else 0)))
            vev = p.op("vector", lambda h, grp=grp: h.tensor_copy(
                vtok[:, grp * 1024:(grp + 1) * 1024], VPT[:]), waits=[tts[-1]])
        vsc = p.op("vector", lambda h: h.tensor_tensor(
            vtok[:].rearrange("p (j d) -> p j d", j=32),
            vtok[:].rearrange("p (j d) -> p j d", j=32),
            fvec[:].unsqueeze(2).broadcast_to([128, 32, 128]), ALU.mult), waits=[vev, f2])
        if stage == 'vtok':
            for qb in range(8):
                qsl = slice(qb * 512, (qb + 1) * 512)
                cp = p.op("vector", lambda h, qsl=qsl: h.tensor_copy(outtmp[:], vtok[:, qsl]),
                          waits=[vsc] + ([prev_store] if prev_store else []))
                prev_store = p.dma("gpsimd", hout[hh][:, qsl], outtmp[:], ost, waits=[cp])
            continue
        mqk = []
        prev_nm = None
        for m, src in enumerate((qt, kt)):
            sq = p.op("scalar", lambda h, src=src: h.activation(sqb[:], src[:], AF.Square),
                      waits=[prev_proj_ev] + ([prev_nm] if prev_nm else []))
            reds = []
            for sl in range(8):
                nm = p.op("tensor", lambda h, sl=sl: h.matmul(
                    PN[0:1, :], ns.ones_bf[:], sqb[:, sl * 512:(sl + 1) * 512],
                    start=True, stop=True), waits=[sq] + ([prev_rb_ev] if sl == 0 else []) +
                    (reds[-1:] if reds else []), inc=1)
                prev_nm = nm
                reds.append(p.op("vector", lambda h, sl=sl: h.reduce_max(
                    srow[:, 4 + sl:5 + sl], PN[0:1, :], AX.X), waits=[nm]))
            mqk.append(p.op("vector", lambda h, m=m: h.reduce_max(
                crow[:, m:m + 1], srow[:, 4:12], AX.X), waits=[reds[-1]]))
        c1 = p.op("vector", lambda h: h.tensor_tensor(crow[:, 3:4], crow[:, 0:1], crow[:, 1:2], ALU.mult),
                  waits=mqk)
        c2 = p.op("scalar", lambda h: h.activation(crow[:, 2:3], crow[:, 3:4], AF.Sqrt), waits=[c1])
        c3 = p.op("vector", lambda h: h.tensor_scalar(crow[:, 2:3], crow[:, 2:3], -ALPHA0, None, ALU.mult),
                  waits=[c2])
        bn = p.op("tensor", lambda h: h.matmul(PN[:, 0:1], ns.ones_row[:], crow[:, 2:3],
                                               start=True, stop=True), waits=[c3], inc=1)
        prev_norm_mm = bn
        ce = p.op("vector", lambda h: h.tensor_copy(negCb[:], PN[:, 0:1]), waits=[bn])
        if stage == 'norms':
            for qb in range(8):
                qsl = slice(qb * 512, (qb + 1) * 512)
                cp = p.op("vector", lambda h, qsl=qsl: h.tensor_copy(outtmp[:], qt[:, qsl]),
                          waits=[ce] + ([prev_store] if prev_store else []))
                prev_store = p.dma("gpsimd", hout[hh][:, qsl], outtmp[:], ost, waits=[cp])
            continue
        for qb in range(8):
            qsl = slice(qb * 512, (qb + 1) * 512)
            for pp in range(16):
                par = pp % 2
                smm = [None, None]
                for half in range(2):
                    kc = pp * 2 + half
                    smm[half] = p.op("tensor", lambda h, kc=kc, par=par, half=half, qsl=qsl: h.matmul(
                        PS[:, par * 1024 + half * 512: par * 1024 + (half + 1) * 512],
                        kt[:, kc * 128:(kc + 1) * 128], qt[:, qsl], start=True, stop=True),
                        waits=([ps_free[par]] if half == 0 else []), inc=(1 if half == 1 else 0))
                ex = p.op("scalar", lambda h, par=par: h.activation(
                    ebuf[par][:], PS[:, par * 1024:(par + 1) * 1024], AF.Exp,
                    scale=alphab[:], bias=negCb[:]),
                    waits=[smm[1], ce] + ([eb_free[par]] if eb_free[par] else []))
                ps_free[par] = ex
                for half in range(2):
                    kc = pp * 2 + half
                    p.op("tensor", lambda h, kc=kc, par=par, half=half: h.matmul(
                        PO[:], vtok[:, kc * 128:(kc + 1) * 128],
                        ebuf[par][:, half * 512:(half + 1) * 512],
                        start=(kc == 0), stop=(kc == 31)),
                        waits=([ex, vsc] if half == 0 else []) + ([prev_store] if (pp == 0 and half == 0 and prev_store) else []),
                        inc=0)
                    dn_t = p.op("tensor", lambda h, kc=kc, par=par, half=half: h.matmul(
                        PD[:], fvecb[:, kc:kc + 1], ebuf[par][:, half * 512:(half + 1) * 512],
                        start=(kc == 0), stop=(kc == 31)), waits=[], inc=(1 if half == 1 else 0))
                if dn_t is None:
                    raise AssertionError
                eb_free[par] = dn_t
            r1 = p.op("scalar", lambda h: h.activation(rrow[:], PD[:], AF.Ln), waits=[eb_free[1]])
            r2 = p.op("scalar", lambda h: h.activation(rrow[:], rrow[:], AF.Exp, scale=-1.0))
            rbm = p.op("tensor", lambda h: h.matmul(PN[:], ns.ones_row[:], rrow[:],
                                                    start=True, stop=True),
                       waits=[r2, prev_rb_ev], inc=1)
            rbe = p.op("vector", lambda h: h.tensor_copy(rb[:], PN[:]), waits=[rbm])
            prev_rb_ev = rbe
            oev = p.op("vector", lambda h: h.tensor_tensor(outtmp[:], PO[:], rb[:], ALU.mult),
                       waits=[rbe, eb_free[1]] + ([prev_store] if prev_store else []))
            prev_store = p.dma("gpsimd", hout[hh][:, qsl], outtmp[:], ost, waits=[oev])
    p.wait("gpsimd", [("ost", p.sem_val["ost"])])
    return p.build()



def build_proj_residual(scale=1.0, emit_out_ln=True):
    p = Prog("projres")
    ns = emit_common_consts(p)
    hin = p.inp("hin", [16, 128, TPC], BF16)
    w_m = p.inp("w", [2048, 2048], BF16)
    bpre = p.inp("b", [128, 16], F32)
    x_in = p.inp("x", [16, 128, TPC])
    x_out = p.out("xo", [16, 128, TPC])
    lnx_out = p.out("lnxo", [16, 128, TPC], BF16) if emit_out_ln else None

    ld = p.dma_sem("ld"); wld = p.dma_sem("wld"); xst = p.dma_sem("x_st")

    PW = p.psum("PW", [128, 1024], F32)
    PR = p.psum("PR", [128, 2048], F32)

    hsb = p.sbuf("hsb", [128, 16 * TPC], BF16)
    wsb = [p.sbuf(f"wsb{i}", [128, 16 * 128], BF16) for i in range(2)]
    bsb = p.sbuf("bsb", [128, 16], F32)
    xo_t = p.sbuf("xo_t", [128, 512], F32)
    xn = p.sbuf("xn", [128, 16 * 512], F32)
    tmp_o = p.sbuf("tmp_o", [128, 512], F32)
    xsq = p.sbuf("xsq", [128, 512], BF16)
    mu_row = p.sbuf("mu_row", [1, 512], F32)
    rstd_row = p.sbuf("rstd_row", [1, 512], F32)
    tmp_row = p.sbuf("tmp_row", [1, 512], F32)
    omu_b = p.sbuf("omu_b", [128, 512], F32)
    orstd_b = p.sbuf("orstd_b", [128, 512], F32)
    lnxo_t = p.sbuf("lnxo_t", [128, 512], BF16)

    sums_ps = PR[0:1, 0:512]
    sumsq_ps = PR[0:1, 512:1024]
    bc_ps = PR[:, 1024:1536]

    for k in range(16):
        t_h = p.dma("sync", hsb[:, k * TPC:(k + 1) * TPC], hin[k], ld)
    t_b = p.dma("sync", bsb[:], bpre, ld)
    h_ready = t_b
    w_v = w_m.rearrange("(k p) o -> p k o", p=128)

    prev_w_use = [None, None]
    prev_ev = [None, None]
    last_xo_read = None
    prev_of = None
    last_bcast_ev = None
    ln_store = None
    tmp_free = None
    xn_store = [None] * 16
    o_mm_toks = []
    for s in range(4):
        tsl = slice(s * 512, (s + 1) * 512)
        o_mm_toks = []
        for dout in range(16):
            par = dout % 2
            tw = p.dma("sync", wsb[par][:].rearrange("p (k o) -> p k o", k=16),
                       w_v[:, :, dout * 128:(dout + 1) * 128], wld,
                       waits=[prev_w_use[par]])
            mmt = None
            for k in range(16):
                mmt = p.op("tensor", lambda h, k=k, par=par, tsl=tsl: h.matmul(
                    PW[:, par * 512:(par + 1) * 512], wsb[par][:, k * 128:(k + 1) * 128],
                    hsb[:, k * TPC:(k + 1) * TPC][:, tsl], start=(k == 0), stop=(k == 15)),
                    waits=([tw, h_ready] + ([prev_ev[par]] if prev_ev[par] else [])) if k == 0 else (),
                    inc=(1 if k == 15 else 0))
            prev_w_use[par] = mmt
            txo = p.dma("sync", xo_t[:], x_in[dout][:, tsl], ld,
                        waits=[last_xo_read])
            m1 = p.op("scalar", lambda h, par=par, dout=dout: h.activation(
                tmp_o[:], PW[:, par * 512:(par + 1) * 512], AF.Identity,
                scale=float(scale), bias=bsb[:, dout:dout + 1]),
                waits=[mmt, t_b] + ([tmp_free] if tmp_free else []))
            prev_ev[par] = m1
            m2 = p.op("vector", lambda h, dout=dout: h.tensor_tensor(
                xn[:, dout * 512:(dout + 1) * 512], tmp_o[:], xo_t[:], ALU.add),
                waits=[txo, m1] + ([xn_store[dout]] if xn_store[dout] else []))
            last_xo_read = m2
            tmp_free = m2
            xn_store[dout] = p.dma("gpsimd", x_out[dout][:, tsl], xn[:, dout * 512:(dout + 1) * 512], xst, waits=[m2])
            if emit_out_ln:
                sq1 = p.op("scalar", lambda h, dout=dout: h.activation(
                    xsq[:], xn[:, dout * 512:(dout + 1) * 512], AF.Square),
                    waits=[m2] + (o_mm_toks[-1:] if o_mm_toks else []))
                p.op("tensor", lambda h, dout=dout: h.matmul(
                    sums_ps[:], ns.ones_f32[:], xn[:, dout * 512:(dout + 1) * 512],
                    start=(dout == 0), stop=(dout == 15)),
                    waits=[m2] + ns.ready + ([prev_of] if prev_of else []), inc=0)
                t = p.op("tensor", lambda h, dout=dout: h.matmul(
                    sumsq_ps[:], ns.ones_bf[:], xsq[:],
                    start=(dout == 0), stop=(dout == 15)), waits=[sq1], inc=1)
                o_mm_toks.append(t)
        if emit_out_ln:
            oc2, os2, oc1 = emit_ln_rows(p, sums_ps, sumsq_ps, mu_row, rstd_row, tmp_row,
                                         D, LN_EPS, 512, waits=[o_mm_toks[-1]])
            prev_of = oc2
            ev_om = emit_bcast_row(p, ns, mu_row, omu_b, bc_ps, 512,
                                   waits=[oc1, oc2] + ([last_bcast_ev] if last_bcast_ev else []))
            ev_or = emit_bcast_row(p, ns, rstd_row, orstd_b, bc_ps, 512, waits=[os2, ev_om])
            last_bcast_ev = ev_or
            for dout in range(16):
                a1 = p.op("vector", lambda h, dout=dout: h.tensor_tensor(
                    tmp_o[:], xn[:, dout * 512:(dout + 1) * 512], omu_b[:], ALU.subtract),
                    waits=[ev_om, ev_or] + ([ln_store] if ln_store else []))
                a2 = p.op("vector", lambda h: h.tensor_tensor(lnxo_t[:], tmp_o[:], orstd_b[:], ALU.mult))
                ln_store = p.dma("gpsimd", lnx_out[dout][:, tsl], lnxo_t[:], xst, waits=[a2])
                tmp_free = a2
    p.wait("gpsimd", [("x_st", p.sem_val["x_st"])])
    return p.build()



def build_moe(debug=False):
    p = Prog("moe")
    ns = emit_common_consts(p)
    x_in = p.inp("x", [16, 128, TPC])
    lnx_in = p.inp("lnx", [16, 128, TPC], BF16)
    routerT = p.inp("routerT", [2048, 8], BF16)
    gateT = p.inp("gateT", [8, 2048, 1408], BF16)
    upT = p.inp("upT", [8, 2048, 1408], BF16)
    downT = p.inp("downT", [8, 1408, 2048], BF16)
    x_out = p.out("xo", [16, 128, TPC])
    lnx_out = p.out("lnxo", [16, 128, TPC], BF16)
    acc = p.dram("acc", [16, 128, TPC], F32)
    if debug:
        dbg_lgt = p.out("dbg_lgt", [128, 128], F32)
        dbg_gr = p.out("dbg_gr", [8, TPC], F32)
        dbg_acc = p.out("dbg_acc", [16, 128, TPC], F32)
        dbg_act = p.out("dbg_act", [11, 128, TPC], BF16)
        dbg_gb = p.out("dbg_gb", [128, TPC], F32)

    ld = p.dma_sem("ld"); wld = p.dma_sem("wld"); ast_ = p.dma_sem("acc_st")
    xst = p.dma_sem("x_st")

    P1 = p.psum("P1", [128, 2048], F32)
    P2 = p.psum("P2", [128, 2048], F32)

    _scoped_names = []
    def sc(name, shape, dt):
        _scoped_names.append(name)
        return p.sbuf_scoped(name, shape, dt)
    sc2 = sc
    lnx = sc("lnxs", [128, 16 * TPC], BF16)
    rsb = sc("rsb", [128, 16 * 8], BF16)
    lgf = sc("lgf", [8, TPC], F32)
    lgt = sc("lgt", [128, 16 * 8], F32)
    m1 = sc("m1t", [128, 16], F32)
    m2 = sc("m2t", [128, 16], F32)
    msk = sc("mskt", [128, 16 * 8], F32)
    mskd = sc("mskd", [128, 16 * 8], F32)
    gwt = sc("gwt", [128, 16], F32)
    geall = sc("geall", [128, 16 * 8], F32)
    gtr = sc("gtr", [16, 128], F32)
    g_rows = sc("g_rows", [8, TPC], F32)
    g_b = sc("g_b", [128, TPC], F32)
    wg_sb = [sc(f"wg_sb{i}", [128, 16 * 128], BF16) for i in range(2)]
    wu_sb = [sc(f"wu_sb{i}", [128, 16 * 128], BF16) for i in range(2)]
    wd_sb = [sc(f"wd_sb{i}", [128, 11 * 128], BF16) for i in range(2)]
    sg = sc("sg", [128, TPC], F32)
    upg = sc("upg", [128, TPC], F32)
    act_sb = sc("act_sb", [128, 11 * TPC], BF16)
    stage = sc("stage", [128, TPC], F32)
    idbf32 = sc("idbf32", [128, 128], F32)
    cid32 = p.const("c_id_f32", np.eye(128, dtype=np.float32))

    for k in range(16):
        t_lnx = p.dma("sync", lnx[:, k * TPC:(k + 1) * TPC], lnx_in[k], ld)
    t_r = p.dma("sync", rsb[:].rearrange("p (k e) -> p k e", k=16),
                routerT.rearrange("(k p) e -> p k e", p=128), ld)
    t_id = p.dma("sync", idbf32[:], cid32.ap(), ld)
    lnx_ready = t_id

    lg_mm = None
    for k in range(16):
        for t in range(4):
            sl = slice(t * 512, (t + 1) * 512)
            lg_mm = p.op("tensor", lambda h, k=k, sl=sl: h.matmul(
                P1[0:8, sl], rsb[:, k * 8:(k + 1) * 8],
                lnx[:, k * TPC:(k + 1) * TPC][:, sl], start=(k == 0), stop=(k == 15)),
                waits=([lnx_ready] + ns.ready) if (k == 0 and t == 0) else (),
                inc=(1 if (k == 15 and t == 3) else 0))
    ev_lg = p.op("vector", lambda h: h.tensor_copy(lgf[:], P1[0:8, :]), waits=[lg_mm])
    prev_cp = ev_lg
    for j in range(16):
        tr = p.op("tensor", lambda h, j=j: h.transpose(
            P2[:, 0:8], lgf[0:8, j * 128:(j + 1) * 128], idbf32[0:8, 0:8]),
            waits=[prev_cp, t_id], inc=1)
        prev_cp = p.op("vector", lambda h, j=j: h.tensor_copy(lgt[:, j * 8:(j + 1) * 8], P2[:, 0:8]),
                       waits=[tr])
    lgv = lgt[:].rearrange("p (t e) -> p t e", t=16)
    v1 = p.op("vector", lambda h: h.tensor_reduce(m1[:], lgv, AX.X, ALU.max), waits=[prev_cp])
    v2 = p.op("vector", lambda h: h.tensor_tensor(
        msk[:].rearrange("p (t e) -> p t e", t=16), lgv,
        m1[:].unsqueeze(2).broadcast_to([128, 16, 8]), ALU.is_ge))
    v3 = p.op("vector", lambda h: h.scalar_tensor_tensor(
        mskd[:], msk[:], -1e30, lgt[:], ALU.mult, ALU.add))
    v4 = p.op("vector", lambda h: h.tensor_reduce(m2[:], mskd[:].rearrange("p (t e) -> p t e", t=16),
                                                  AX.X, ALU.max))
    v5 = p.op("vector", lambda h: h.tensor_tensor(gwt[:], m1[:], m2[:], ALU.subtract))
    v6 = p.op("scalar", lambda h: h.activation(gwt[:], gwt[:], AF.Sigmoid), waits=[v5])
    v7 = p.op("vector", lambda h: h.tensor_tensor(
        geall[:].rearrange("p (t e) -> p t e", t=16),
        msk[:].rearrange("p (t e) -> p t e", t=16),
        gwt[:].unsqueeze(2).broadcast_to([128, 16, 8]), ALU.mult), waits=[v6])
    prev = v7
    grow_dma = []
    for e in range(8):
        tr2 = p.op("tensor", lambda h, e=e: h.transpose(
            P2[0:16, 0:128], geall[:].rearrange("p (t f) -> p t f", t=16)[:, :, e], idbf32[:]),
            waits=[prev], inc=1)
        ev2 = p.op("vector", lambda h: h.tensor_copy(gtr[:], P2[0:16, 0:128]),
                   waits=[tr2] + (grow_dma[-1:] if grow_dma else []))
        prev = ev2
        grow_dma.append(p.dma("gpsimd", g_rows[e:e + 1, :].rearrange("a (t q) -> a t q", t=16),
                              gtr[:], ld, waits=[ev2]))
    grows_done = grow_dma[-1]
    if debug:
        d1 = p.dma("gpsimd", dbg_lgt, lgt[:], ld, waits=[v7])
        d2 = p.dma("gpsimd", dbg_gr, g_rows[:], ld, waits=[grows_done])

    gateT_v = gateT.rearrange("e (k p) i -> e p k i", p=128)
    upT_v = upT.rearrange("e (k p) i -> e p k i", p=128)
    downT_v = downT.rearrange("e (k p) o -> e p k o", p=128)

    prev_gb_ev = prev
    prev_w_use = {"g": [None, None], "u": [None, None], "d": [None, None]}
    prev_silu = None
    prev_upg = None
    prev_actw = None
    last_acc_dma = None
    dn_ev_bank = [None, None]
    grow1 = sc2("grow1", [1, TPC], F32)
    prev_g1 = None
    for e in range(8):
        tg1 = p.dma("gpsimd", grow1[:], g_rows[e:e + 1, :], ld,
                    waits=[grows_done] + ([prev_g1] if prev_g1 else []))
        gb_mm = None
        for t in range(4):
            sl = slice(t * 512, (t + 1) * 512)
            gb_mm = p.op("tensor", lambda h, sl=sl: h.matmul(
                P2[:, sl], ns.ones_row[:], grow1[0:1, sl], start=True, stop=True),
                waits=([tg1, prev_gb_ev] +
                       ([prev_upg] if prev_upg else []) +
                       ([dn_ev_bank[1]] if dn_ev_bank[1] else [])) if t == 0 else (),
                inc=(1 if t == 3 else 0))
        gb_ev = p.op("vector", lambda h: h.tensor_copy(g_b[:], P2[:]), waits=[gb_mm])
        prev_gb_ev = gb_ev
        prev_g1 = gb_mm
        for i in range(11):
            par = i % 2
            twg = p.dma("sync", wg_sb[par][:].rearrange("p (k o) -> p k o", k=16),
                        gateT_v[e][:, :, i * 128:(i + 1) * 128], wld,
                        waits=[prev_w_use["g"][par]])
            twu = p.dma("sync", wu_sb[par][:].rearrange("p (k o) -> p k o", k=16),
                        upT_v[e][:, :, i * 128:(i + 1) * 128], wld,
                        waits=[prev_w_use["u"][par]])
            g_mm = None
            u_mm = None
            for k in range(16):
                for t in range(4):
                    sl = slice(t * 512, (t + 1) * 512)
                    g_mm = p.op("tensor", lambda h, k=k, sl=sl, par=par: h.matmul(
                        P1[:, sl], wg_sb[par][:, k * 128:(k + 1) * 128],
                        lnx[:, k * TPC:(k + 1) * TPC][:, sl], start=(k == 0), stop=(k == 15)),
                        waits=([twg] + ([prev_silu] if prev_silu else []) +
                               ([dn_ev_bank[0]] if (i == 0 and dn_ev_bank[0]) else [])) if (k == 0 and t == 0) else (),
                        inc=(1 if (k == 15 and t == 3) else 0))
            for k in range(16):
                for t in range(4):
                    sl = slice(t * 512, (t + 1) * 512)
                    u_mm = p.op("tensor", lambda h, k=k, sl=sl, par=par: h.matmul(
                        P2[:, sl], wu_sb[par][:, k * 128:(k + 1) * 128],
                        lnx[:, k * TPC:(k + 1) * TPC][:, sl], start=(k == 0), stop=(k == 15)),
                        waits=([twu, gb_ev] + ([prev_upg] if prev_upg else [])) if (k == 0 and t == 0) else (),
                        inc=(1 if (k == 15 and t == 3) else 0))
            prev_w_use["g"][par] = g_mm
            prev_w_use["u"][par] = u_mm
            s1 = p.op("scalar", lambda h: h.activation(sg[:], P1[:], AF.Silu), waits=[g_mm])
            prev_silu = s1
            u1 = p.op("vector", lambda h: h.tensor_tensor(upg[:], P2[:], g_b[:], ALU.mult),
                      waits=[u_mm, gb_ev])
            prev_upg = u1
            a1 = p.op("vector", lambda h, i=i: h.tensor_tensor(
                act_sb[:, i * TPC:(i + 1) * TPC], sg[:], upg[:], ALU.mult),
                waits=[s1] + ([prev_actw] if prev_actw else []))
            prev_actw = a1
        if debug and e == 0:
            for ii in range(11):
                p.dma("gpsimd", dbg_act[ii], act_sb[:, ii * TPC:(ii + 1) * TPC], ld,
                      waits=[prev_actw])
            p.dma("gpsimd", dbg_gb, g_b[:], ld, waits=[prev_upg])
        for dout in range(16):
            par = dout % 2
            bank = P1 if par == 0 else P2
            twd = p.dma("sync", wd_sb[par][:].rearrange("p (k o) -> p k o", k=11),
                        downT_v[e][:, :, dout * 128:(dout + 1) * 128], wld,
                        waits=[prev_w_use["d"][par]])
            d_mm = None
            for i in range(11):
                for t in range(4):
                    sl = slice(t * 512, (t + 1) * 512)
                    d_mm = p.op("tensor", lambda h, i=i, sl=sl, bank=bank, par=par: h.matmul(
                        bank[:, sl], wd_sb[par][:, i * 128:(i + 1) * 128],
                        act_sb[:, i * TPC:(i + 1) * TPC][:, sl], start=(i == 0), stop=(i == 10)),
                        waits=([twd, prev_actw, prev_silu, prev_upg] +
                               ([dn_ev_bank[par]] if dn_ev_bank[par] else [])) if (i == 0 and t == 0) else (),
                        inc=(1 if (i == 10 and t == 3) else 0))
            prev_w_use["d"][par] = d_mm
            dev = p.op("vector", lambda h, bank=bank: h.tensor_copy(stage[:], bank[:]),
                       waits=[d_mm] + ([last_acc_dma] if last_acc_dma else []))
            dn_ev_bank[par] = dev
            if e == 0:
                last_acc_dma = p.dma("gpsimd", acc[dout], stage[:], ast_, waits=[dev])
            else:
                last_acc_dma = p.dma("gpsimd", acc[dout], stage[:], ast_, waits=[dev],
                                     accum_op=ALU.add)
        if e == 0:
            dbg_acc_after = last_acc_dma
    acc_done = last_acc_dma

    if debug:
        for kk in range(16):
            p.dma("sync", dbg_acc[kk], acc[kk], ld, waits=[dbg_acc_after])
    for nm in reversed(_scoped_names):
        p.sbuf_unscope(nm)
    xo_t = p.sbuf("xo_t", [128, 512], F32)
    ac_t = p.sbuf("ac_t", [128, 512], F32)
    xn = p.sbuf("xn", [128, 16 * 512], F32)
    xsq = p.sbuf("xsq", [128, 512], BF16)
    mu_row = p.sbuf("mu_row", [1, 512], F32)
    rstd_row = p.sbuf("rstd_row", [1, 512], F32)
    tmp_row = p.sbuf("tmp_row", [1, 512], F32)
    omu_b = p.sbuf("omu_b", [128, 512], F32)
    orstd_b = p.sbuf("orstd_b", [128, 512], F32)
    tmp_o = p.sbuf("tmp_o", [128, 512], F32)
    lnxo_t = p.sbuf("lnxo_t", [128, 512], BF16)
    sums_ps = P1[0:1, 0:512]
    sumsq_ps = P1[0:1, 512:1024]
    bc_ps = P1[:, 1024:1536]

    prev_of = None
    last_bcast_ev = None
    ln_store = None
    xn_store = [None] * 16
    prev_xload = None
    for s in range(4):
        tsl = slice(s * 512, (s + 1) * 512)
        o_mm_toks = []
        for dout in range(16):
            tx = p.dma("sync", xo_t[:], x_in[dout][:, tsl], ld,
                       waits=[acc_done] + ([prev_xload] if prev_xload else []))
            ta = p.dma("sync", ac_t[:], acc[dout][:, tsl], ld)
            m2t = p.op("vector", lambda h, dout=dout: h.scalar_tensor_tensor(
                xn[:, dout * 512:(dout + 1) * 512], ac_t[:], 0.5, xo_t[:], ALU.mult, ALU.add),
                waits=[tx, ta] + ([xn_store[dout]] if xn_store[dout] else []))
            prev_xload = m2t
            xn_store[dout] = p.dma("gpsimd", x_out[dout][:, tsl], xn[:, dout * 512:(dout + 1) * 512],
                                   xst, waits=[m2t])
            sq1 = p.op("scalar", lambda h, dout=dout: h.activation(
                xsq[:], xn[:, dout * 512:(dout + 1) * 512], AF.Square),
                waits=[m2t] + (o_mm_toks[-1:] if o_mm_toks else []) +
                ([prev_silu] if (s == 0 and dout == 0) else []))
            p.op("tensor", lambda h, dout=dout: h.matmul(
                sums_ps[:], ns.ones_f32[:], xn[:, dout * 512:(dout + 1) * 512],
                start=(dout == 0), stop=(dout == 15)),
                waits=[m2t] + ([dn_ev_bank[0], dn_ev_bank[1]] if (s == 0 and dout == 0) else []) +
                ([prev_of] if prev_of else []), inc=0)
            t = p.op("tensor", lambda h, dout=dout: h.matmul(
                sumsq_ps[:], ns.ones_bf[:], xsq[:],
                start=(dout == 0), stop=(dout == 15)), waits=[sq1], inc=1)
            o_mm_toks.append(t)
        oc2, os2, oc1 = emit_ln_rows(p, sums_ps, sumsq_ps, mu_row, rstd_row, tmp_row,
                                     D, LN_EPS, 512, waits=[o_mm_toks[-1]])
        prev_of = oc2
        ev_om = emit_bcast_row(p, ns, mu_row, omu_b, bc_ps, 512,
                               waits=[oc1, oc2] + ([last_bcast_ev] if last_bcast_ev else []))
        ev_or = emit_bcast_row(p, ns, rstd_row, orstd_b, bc_ps, 512, waits=[os2, ev_om])
        last_bcast_ev = ev_or
        for dout in range(16):
            a1 = p.op("vector", lambda h, dout=dout: h.tensor_tensor(
                tmp_o[:], xn[:, dout * 512:(dout + 1) * 512], omu_b[:], ALU.subtract),
                waits=[ev_om, ev_or] + ([ln_store] if ln_store else []))
            a2 = p.op("vector", lambda h: h.tensor_tensor(lnxo_t[:], tmp_o[:], orstd_b[:], ALU.mult))
            ln_store = p.dma("gpsimd", lnx_out[dout][:, tsl], lnxo_t[:], xst, waits=[a2])
    p.wait("gpsimd", [("x_st", p.sem_val["x_st"])])
    return p.build()


_BUILD_CACHE = {}


def _cached(name, fn):
    if name not in _BUILD_CACHE:
        _BUILD_CACHE[name] = fn()
    return _BUILD_CACHE[name]


def run_moe(inp, w, x_fm, lnx, _run):
    nc = _cached("moe", build_moe)
    in_maps = [{"x": x_fm[c], "lnx": lnx[c], "routerT": w.routerT,
                "gateT": w.gateT, "upT": w.upT, "downT": w.downT} for c in range(NC)]
    res = _run(nc, in_maps, core_ids=list(range(NC)))
    return [res.results[c]["xo"] for c in range(NC)], [res.results[c]["lnxo"] for c in range(NC)]




NS_ABC = ((4.0848, -6.8946, 2.927), (3.9505, -6.3029, 2.6377),
          (3.7418, -5.5913, 2.3037), (2.8769, -3.1427, 1.2046),
          (2.8366, -3.0525, 1.2012))
BASE_LR_INV = float(0.01 + np.log(-np.expm1(np.float32(-0.01))))


def build_l1_attn_core(stage='full'):
    p = Prog("l1attn")
    ns = emit_common_consts(p)
    S = 4096
    lnx_own_in = p.inp("lnx_own", [16, 128, TPC], BF16)
    lnx_full_in = p.inp("lnx_full", [16, 128, S], BF16)
    wqkv = p.inp("wqkv", [2048, 6144], BF16)
    waux = p.inp("waux", [2048, 40], BF16)
    auxb = p.inp("auxb", [1, 40], F32)
    lngb_in = p.inp("lngb", [1, 256], F32)
    w0_in = p.inp("w0", [8, 128, 128])
    w1_in = p.inp("w1", [8, 128, 128])
    w2_in = p.inp("w2", [8, 128, 128])
    sout = p.out("sout", [8, 128, S], BF16)
    qkv_dram = p.dram("qkv_scr", [TPC, 6144], BF16)

    ld = p.dma_sem("ld"); wld = p.dma_sem("wld"); qst = p.dma_sem("qst")
    qld = p.dma_sem("qld"); ost = p.dma_sem("ost"); wsl = p.dma_sem("wsl")

    PA = p.psum("PA", [128, 2048], F32)
    PB = p.psum("PB", [128, 1536], F32)
    PT = p.psum("PT", [128, 1024], BF16)

    aux_sb = p.sbuf("aux_sb", [128, 32 * 40], F32)
    sp_t = p.sbuf("sp_t", [128, 32 * 24], F32)
    auxr = p.sbuf("auxr", [1, 40], F32)
    lng_b = p.sbuf("lng_b", [128, 128], F32)
    lnb_b = p.sbuf("lnb_b", [128, 128], F32)
    lngr = p.sbuf("lngr", [1, 256], F32)
    idbf = p.sbuf("idbf", [128, 128], BF16)
    cidb = p.const("c_idb", np.eye(128, dtype=BF))

    t_id = p.dma("sync", idbf[:], cidb.ap(), ld)
    t_lg = p.dma("sync", lngr[:], lngb_in, ld)
    t_ab = p.dma("sync", auxr[:], auxb, ld)

    lnxo = p.sbuf_scoped("lnxo", [128, 16 * TPC], BF16)
    wcol = p.sbuf_scoped("wcol", [128, 16 * 512], BF16)
    qstage = [p.sbuf_scoped(f"qstage{i}", [128, 512], BF16) for i in range(2)]
    for k in range(16):
        t_lo = p.dma("sync", lnxo[:, k * TPC:(k + 1) * TPC], lnx_own_in[k], ld)
    lnx_ready = t_lo

    wqkv_v = wqkv.rearrange("(k p) o -> p k o", p=128)
    prev_stage = [None, None]
    bank_ev = [None] * 4
    prev_wcol_use = None
    for cn in range(12):
        osl = slice(cn * 512, (cn + 1) * 512)
        tw = p.dma("sync", wcol[:].rearrange("p (k o) -> p k o", k=16), wqkv_v[:, :, osl], wld,
                   waits=[prev_wcol_use] if prev_wcol_use else [])
        for tt in range(16):
            par = tt % 2
            bk = tt % 4
            mmq = None
            for k in range(16):
                mmq = p.op("tensor", lambda h, k=k, tt=tt, bk=bk: h.matmul(
                    PA[:, bk * 512:(bk + 1) * 512],
                    lnxo[:, k * TPC:(k + 1) * TPC][:, tt * 128:(tt + 1) * 128],
                    wcol[:, k * 512:(k + 1) * 512], start=(k == 0), stop=(k == 15)),
                    waits=([tw, lnx_ready] + ([bank_ev[bk]] if bank_ev[bk] else [])) if k == 0 else (),
                    inc=(1 if k == 15 else 0))
            ev = p.op("scalar", lambda h, par=par, bk=bk: h.activation(
                qstage[par][:], PA[:, bk * 512:(bk + 1) * 512], AF.Silu),
                waits=[mmq] + ([prev_stage[par]] if prev_stage[par] else []))
            bank_ev[bk] = ev
            prev_stage[par] = p.dma("gpsimd", qkv_dram[tt * 128:(tt + 1) * 128, osl],
                                    qstage[par][:], qst, waits=[ev])
        prev_wcol_use = mmq
    qkv_done = prev_stage[par]

    if stage == 'qkv':
        stq = None
        for j in range(8):
            stq = p.dma("gpsimd", sout[j], qkv_dram[256 * j:256 * j + 128, 0:4096], ost,
                        waits=[qkv_done])
        p.wait("gpsimd", [("ost", p.sem_val["ost"])])
        return p.build()
    for nm in ("qstage1", "qstage0", "wcol", "lnxo"):
        p.sbuf_unscope(nm)
    lnx = p.sbuf_scoped("lnxs", [128, 16 * S], BF16)
    t_lf = None
    for k in range(16):
        t_lf = p.dma("sync", lnx[:, k * S:(k + 1) * S], lnx_full_in[k], ld,
                     waits=[qkv_done] if k < 2 else [])
    lnxf_ready = t_lf
    wax = p.sbuf_scoped("wax", [128, 16 * 40], BF16)
    t_wax = p.dma("sync", wax[:].rearrange("p (k o) -> p k o", k=16),
                  waux.rearrange("(k p) o -> p k o", p=128), wld)
    prev_auxev = None
    for tt in range(32):
        boff = 512 * (tt % 2)
        mm = None
        for k in range(16):
            mm = p.op("tensor", lambda h, k=k, tt=tt, boff=boff: h.matmul(
                PB[:, boff:boff + 40], lnx[:, k * S:(k + 1) * S][:, tt * 128:(tt + 1) * 128],
                wax[:, k * 40:(k + 1) * 40], start=(k == 0), stop=False),
                waits=([t_wax, t_ab, lnxf_ready] + ([prev_auxev] if prev_auxev else [])) if k == 0 else (), inc=0)
        mm = p.op("tensor", lambda h, boff=boff: h.matmul(
            PB[:, boff:boff + 40], ns.ones_row[:], auxr[:], start=False, stop=True),
            waits=ns.ready if tt == 0 else [], inc=1)
        ev = p.op("vector", lambda h, tt=tt, boff=boff: h.tensor_copy(
            aux_sb[:, tt * 40:(tt + 1) * 40], PB[:, boff:boff + 40]), waits=[mm])
        prev_auxev = ev
    aux_v = aux_sb[:].rearrange("p (t c) -> p t c", t=32)
    sp_v = sp_t[:].rearrange("p (t c) -> p t c", t=32)
    n1 = p.op("scalar", lambda h: h.activation(sp_v, aux_v[:, :, 0:24], AF.Abs), waits=[prev_auxev])
    n2 = p.op("scalar", lambda h: h.activation(sp_v, sp_v, AF.Exp, scale=-1.0))
    n3 = p.op("scalar", lambda h: h.activation(sp_v, sp_v, AF.Ln, bias=1.0))
    n4 = p.op("vector", lambda h: h.tensor_scalar_max(aux_v[:, :, 0:24], aux_v[:, :, 0:24], 0.0),
              waits=[n1])
    n5 = p.op("vector", lambda h: h.tensor_tensor(aux_v[:, :, 0:24], aux_v[:, :, 0:24], sp_v, ALU.add),
              waits=[n3])
    n6 = p.op("scalar", lambda h: h.activation(aux_v[:, :, 24:32], aux_v[:, :, 24:32], AF.Sigmoid),
              waits=[n5])
    n7 = p.op("scalar", lambda h: h.activation(aux_v[:, :, 32:40], aux_v[:, :, 32:40], AF.Silu))
    aux_done = n7
    aux_mm_done = prev_auxev
    b1 = p.op("tensor", lambda h: h.matmul(PB[:, 0:128], ns.ones_row[:], lngr[:, 0:128],
                                           start=True, stop=True), waits=[t_lg, prev_auxev], inc=1)
    e1 = p.op("vector", lambda h: h.tensor_copy(lng_b[:], PB[:, 0:128]), waits=[b1])
    b2 = p.op("tensor", lambda h: h.matmul(PB[:, 0:128], ns.ones_row[:], lngr[:, 128:256],
                                           start=True, stop=True), waits=[e1], inc=1)
    e2 = p.op("vector", lambda h: h.tensor_copy(lnb_b[:], PB[:, 0:128]), waits=[b2])
    consts_done = e2

    if stage == 'aux':
        stq = p.dma("gpsimd", sout[0][:, 0:1280], aux_sb[:], ost, waits=[aux_done, consts_done])
        p.wait("gpsimd", [("ost", p.sem_val["ost"])])
        return p.build()
    for nm in ("wax", "lnxs"):
        p.sbuf_unscope(nm)

    qpos = p.sbuf("qpos", [128, 32 * 128], BF16)
    kpos = p.sbuf("kpos", [128, 32 * 128], BF16)
    vpos = p.sbuf("vpos", [128, 32 * 128], BF16)
    psq = p.sbuf("psq", [128, 32 * 128], BF16)
    nrm = p.sbuf("nrm", [128, 32], F32)
    qT = p.sbuf("qT", [128, 32 * 128], BF16)
    kT = p.sbuf("kT", [128, 16 * 128], BF16)
    vT = p.sbuf("vT", [128, 16 * 128], BF16)
    w_st = [p.sbuf(f"w_st{i}", [128, 128], F32) for i in range(3)]
    w_bf = [p.sbuf(f"w_bf{i}", [128, 128], BF16) for i in range(3)]
    wT_bf = [p.sbuf(f"wT_bf{i}", [128, 128], BF16) for i in range(3)]
    w_n0 = [p.sbuf(f"w_n0{i}", [128, 1], F32) for i in range(3)]
    gb_s = p.sbuf("gb_s", [128, 16 * 128], F32)
    hb_s = p.sbuf("hb_s", [128, 16 * 128], F32)
    dh_s = p.sbuf("dh_s", [128, 16 * 128], F32)
    sgb = p.sbuf("sgb", [128, 16 * 128], F32)
    ssig = p.sbuf("ssig", [128, 16 * 128], F32)
    hid_s = p.sbuf("hid_s", [128, 16 * 128], F32)
    tmp_s = p.sbuf("tmp_s", [128, 16 * 128], F32)
    dgb_s = p.sbuf("dgb_s", [128, 16 * 128], BF16)
    dhb_s = p.sbuf("dhb_s", [128, 16 * 128], BF16)
    hl_s = p.sbuf("hl_s", [128, 16 * 128], BF16)
    kl0_s = p.sbuf("kl0_s", [128, 16 * 128], BF16)
    kl2_s = p.sbuf("kl2_s", [128, 16 * 128], BF16)
    dwf = [p.sbuf(f"dwf{i}", [128, 128], F32) for i in range(3)]
    xbf = p.sbuf("xbf", [128, 128], BF16)
    xtb = p.sbuf("xtb", [128, 128], BF16)
    a32 = p.sbuf("a32", [128, 128], F32)
    abf = p.sbuf("abf", [128, 128], BF16)
    bmbf = p.sbuf("bmbf", [128, 128], BF16)
    scr1 = p.sbuf("scr1", [128, 128], F32)
    colr = p.sbuf("colr", [128, 1], F32)
    srow = p.sbuf("srow", [1, 8], F32)
    fab = p.sbuf("fab", [128, 1], F32)
    sg_ap = p.sbuf("sg_ap", [128, 2048], F32)
    gh_bf = p.sbuf("gh_bf", [128, 2048], BF16)
    opos = p.sbuf("opos", [128, 16 * 128], F32)
    osq = p.sbuf("osq", [128, 16 * 128], BF16)
    mu_t = p.sbuf("mu_t", [128, 16], F32)
    var_t = p.sbuf("var_t", [128, 16], F32)
    rstd_t = p.sbuf("rstd_t", [128, 16], F32)
    obf = p.sbuf("obf", [128, 32 * 128], BF16)
    oevb = p.sbuf("oevb", [128, 1024], BF16)

    qkv_v = qkv_dram.rearrange("(blk pt tt) (hh c) -> blk pt (tt hh) c", blk=8, tt=8, c=384)

    guards = {"q": None, "k": None, "v": None, "w": None}
    out_dma = None
    obf_free = None
    prev_pt_ev = None

    def pe_t_batch(src, dst, tiles, wait_first, tag):
        nonlocal prev_pt_ev
        for grp in range((tiles + 7) // 8):
            n8 = min(8, tiles - grp * 8)
            last_t = None
            for u in range(n8):
                ti = grp * 8 + u
                last_t = p.op("tensor", lambda h, ti=ti, u=u: h.transpose(
                    PT[:, u * 128:(u + 1) * 128], src[:, ti * 128:(ti + 1) * 128], idbf[:]),
                    waits=(wait_first + ([prev_pt_ev] if prev_pt_ev else [])) if u == 0 else (),
                    inc=(1 if u == n8 - 1 else 0))
            prev_pt_ev = p.op("vector", lambda h, grp=grp, n8=n8: h.tensor_copy(
                dst[:, grp * 1024:grp * 1024 + n8 * 128], PT[:, 0:n8 * 128]), waits=[last_t])
        return prev_pt_ev

    for j in range(8):
        lr0c = aux_v[:, 0:16, 0 + j:1 + j]
        lr1c = aux_v[:, 0:16, 8 + j:9 + j]
        lr2c = aux_v[:, 0:16, 16 + j:17 + j]

        t_q = p.dma("sync", qpos[:].rearrange("p (pt c) -> p pt c", c=128),
                    qkv_v[j, :, :, 0:128].transpose([1, 0, 2]), qld,
                    waits=([qkv_done, aux_mm_done, aux_done, consts_done] if j == 0 else []) +
                    ([guards["q"]] if guards["q"] else []))
        t_k = p.dma("sync", kpos[:].rearrange("p (pt c) -> p pt c", c=128),
                    qkv_v[j, :, :, 128:256].transpose([1, 0, 2]), qld,
                    waits=[guards["k"]] if guards["k"] else [])
        t_v = p.dma("sync", vpos[:].rearrange("p (pt c) -> p pt c", c=128),
                    qkv_v[j, :, :, 256:384].transpose([1, 0, 2]), qld,
                    waits=[guards["v"]] if guards["v"] else [])
        prev_nops = []
        for nm_, src, tok in (("q", qpos, t_q), ("k", kpos, t_k)):
            sq1 = p.op("scalar", lambda h, src=src: h.activation(psq[:], src[:], AF.Square),
                       waits=[tok] + prev_nops)
            r1 = p.op("vector", lambda h: h.tensor_reduce(
                nrm[:], psq[:].rearrange("p (t c) -> p t c", c=128), AX.X, ALU.add),
                waits=[sq1])
            r2 = p.op("scalar", lambda h: h.activation(nrm[:], nrm[:], AF.Sqrt), waits=[r1])
            r3 = p.op("vector", lambda h: h.tensor_scalar_add(nrm[:], nrm[:], 1e-5), waits=[r2])
            r4 = p.op("vector", lambda h: h.reciprocal(nrm[:], nrm[:]))
            r5 = p.op("vector", lambda h, src=src: h.tensor_tensor(
                src[:].rearrange("p (t c) -> p t c", c=128),
                src[:].rearrange("p (t c) -> p t c", c=128),
                nrm[:].unsqueeze(2).broadcast_to([128, 32, 128]), ALU.mult))
            prev_nops = [r5]
        norm_done = prev_nops[0]
        tq_T = pe_t_batch(qpos, qT, 32, [norm_done], "qT")
        tk_T = pe_t_batch(kpos, kT, 16, [norm_done], "kT")
        tv_T = pe_t_batch(vpos, vT, 16, [t_v], "vT")
        if stage == 'loads':
            cp1 = p.op("vector", lambda h: h.tensor_copy(obf[:], qpos[:]),
                       waits=[norm_done, tq_T, tk_T, tv_T] + ([out_dma] if out_dma else []))
            out_dma = p.dma("gpsimd", sout[j], obf[:], ost, waits=[cp1])
            guards["q"] = out_dma; guards["k"] = out_dma; guards["v"] = out_dma; obf_free = out_dma
            continue
        winit = []
        for i, wi in enumerate((w0_in, w1_in, w2_in)):
            t_w = p.dma("sync", w_st[i][:], wi[j], wsl,
                        waits=(([guards["w"], guards["w2"]] if guards["w"] else []) if (j > 0 and i == 0) else []) +
                        ([aux_mm_done] if j == 0 else []))
            c1 = p.op("vector", lambda h, i=i: h.tensor_copy(w_bf[i][:], w_st[i][:]), waits=[t_w])
            sq = p.op("scalar", lambda h, i=i: h.activation(scr1[:], w_st[i][:], AF.Square),
                      waits=[t_w] + (winit[-1:] if winit else []))
            rr = p.op("vector", lambda h: h.tensor_reduce(colr[:], scr1[:], AX.X, ALU.add), waits=[sq])
            rs = p.op("scalar", lambda h, i=i: h.activation(w_n0[i][:], colr[:], AF.Sqrt), waits=[rr])
            tt_ = pe_t_batch(w_bf[i], wT_bf[i], 1, [c1], f"wT{i}")
            winit.append(rs)
        winit_done = [winit[-1], prev_pt_ev]

        def apply_chunk(ch, wg):
            qsl = slice(ch * 2048, (ch + 1) * 2048)
            g_mm = None
            for t4 in range(4):
                g_mm = p.op("tensor", lambda h, t4=t4, qsl=qsl: h.matmul(
                    PA[:, t4 * 512:(t4 + 1) * 512], wT_bf[0][:],
                    qT[:, qsl][:, t4 * 512:(t4 + 1) * 512], start=True, stop=True),
                    waits=wg if t4 == 0 else (), inc=(1 if t4 == 3 else 0))
            sil = p.op("scalar", lambda h: h.activation(sg_ap[:], PA[:], AF.Silu), waits=[g_mm])
            h_mm = None
            for t4 in range(4):
                h_mm = p.op("tensor", lambda h, t4=t4, qsl=qsl: h.matmul(
                    PA[:, t4 * 512:(t4 + 1) * 512], wT_bf[2][:],
                    qT[:, qsl][:, t4 * 512:(t4 + 1) * 512], start=True, stop=True),
                    waits=[sil] if t4 == 0 else (), inc=(1 if t4 == 3 else 0))
            gh = p.op("vector", lambda h: h.tensor_tensor(gh_bf[:], sg_ap[:], PA[:], ALU.mult),
                      waits=[h_mm, sil])
            if stage == 'hpath':
                hcp = p.op("vector", lambda h: h.tensor_copy(opos[:], PA[:]), waits=[h_mm])
                hc2 = p.op("vector", lambda h: h.tensor_copy(obf[:, 0:2048], gh_bf[:]), waits=[gh])
                return hcp, hc2, hcp
            o_mm = None
            for pt in range(16):
                o_mm = p.op("tensor", lambda h, pt=pt: h.matmul(
                    PA[:, pt * 128:(pt + 1) * 128], gh_bf[:, pt * 128:(pt + 1) * 128],
                    wT_bf[1][:], start=True, stop=True),
                    waits=[gh] if pt == 0 else (), inc=(1 if pt == 15 else 0))
            oev_ = p.op("vector", lambda h: h.tensor_copy(opos[:], PA[:]), waits=[o_mm])
            if stage == 'rawout':
                return oev_, o_mm, oev_

            ov = opos[:].rearrange("p (t c) -> p t c", c=128)
            s1 = p.op("scalar", lambda h: h.activation(osq[:], opos[:], AF.Square), waits=[oev_])
            m1_ = p.op("vector", lambda h: h.tensor_reduce(mu_t[:], ov, AX.X, ALU.add), waits=[oev_])
            m2_ = p.op("vector", lambda h: h.tensor_reduce(
                var_t[:], osq[:].rearrange("p (t c) -> p t c", c=128), AX.X, ALU.add), waits=[s1])
            m3 = p.op("vector", lambda h: h.tensor_scalar_mul(mu_t[:], mu_t[:], 1.0 / 128))
            m4 = p.op("vector", lambda h: h.tensor_tensor(rstd_t[:], mu_t[:], mu_t[:], ALU.mult))
            m5 = p.op("vector", lambda h: h.scalar_tensor_tensor(var_t[:], var_t[:], 1.0 / 128,
                                                                rstd_t[:], ALU.mult, ALU.subtract))
            m6 = p.op("scalar", lambda h: h.activation(var_t[:], var_t[:], AF.Ln, bias=1e-6), waits=[m5])
            m7 = p.op("scalar", lambda h: h.activation(rstd_t[:], var_t[:], AF.Exp, scale=-0.5))
            if stage == 'lnstats':
                z1 = p.op("vector", lambda h: h.tensor_copy(obf[:, 0:16], mu_t[:]), waits=[m7])
                z2 = p.op("vector", lambda h: h.tensor_copy(obf[:, 16:32], rstd_t[:]))
                z3 = p.op("vector", lambda h: h.tensor_copy(obf[:, 32:2048 + 32], opos[:]))
                return z3, o_mm, oev_
            a1 = p.op("vector", lambda h: h.tensor_tensor(
                ov, ov, mu_t[:].unsqueeze(2).broadcast_to([128, 16, 128]), ALU.subtract),
                waits=[m7, m2_])
            a2 = p.op("vector", lambda h: h.tensor_tensor(
                ov, ov, rstd_t[:].unsqueeze(2).broadcast_to([128, 16, 128]), ALU.mult))
            a3 = p.op("vector", lambda h: h.tensor_tensor(
                ov, ov, lng_b[:].unsqueeze(1).broadcast_to([128, 16, 128]), ALU.mult))
            a4 = p.op("vector", lambda h: h.tensor_tensor(
                ov, ov, lnb_b[:].unsqueeze(1).broadcast_to([128, 16, 128]), ALU.add))
            if stage == 'lnapplied':
                z1 = p.op("vector", lambda h: h.tensor_copy(obf[:, 0:2048], opos[:]), waits=[a4])
                z2 = p.op("vector", lambda h: h.tensor_copy(obf[:, 2048:2176], lng_b[:]))
                z3 = p.op("vector", lambda h: h.tensor_copy(obf[:, 2176:2304], lnb_b[:]))
                z4 = p.op("vector", lambda h, j=j: h.tensor_copy(
                    obf[:, 2304:2320], aux_v[:, 0:16, 32 + j]))
                return z4, o_mm, oev_
            a5 = p.op("vector", lambda h, ch=ch, j=j: h.tensor_tensor(
                obf[:, ch * 2048:(ch + 1) * 2048].rearrange("p (t c) -> p t c", c=128), ov,
                aux_v[:, 16 * ch:16 * (ch + 1), 32 + j:33 + j].broadcast_to([128, 16, 128]),
                ALU.mult), waits=[aux_done] + ([obf_free] if obf_free else []))
            return a5, o_mm, oev_

        ap0, ap0mm, ap0ev = apply_chunk(0, [tq_T, winit[-1], prev_pt_ev, consts_done])

        if stage == 'apply0':
            cp1 = p.op("vector", lambda h: h.tensor_copy(obf[:, 2048:4096], obf[:, 0:2048]),
                       waits=[ap0] + ([out_dma] if out_dma else []))
            out_dma = p.dma("gpsimd", sout[j], obf[:], ost, waits=[cp1])
            guards["q"] = out_dma; guards["k"] = out_dma; guards["v"] = out_dma; obf_free = out_dma
            continue
        if stage in ('lnstats', 'lnapplied'):
            out_dma = p.dma("gpsimd", sout[j], obf[:], ost,
                            waits=[ap0] + ([out_dma] if out_dma else []))
            guards["q"] = out_dma; guards["k"] = out_dma; guards["v"] = out_dma; obf_free = out_dma
            if stage == 'lnapplied':
                p.dma("gpsimd", sout[1][:, 0:1280], aux_sb[:], ost, waits=[ap0])
                break
            continue
        if stage == 'hpath':
            hcp, hc2 = ap0, ap0mm
            cp1 = p.op("vector", lambda h: h.tensor_copy(obf[:, 2048:4096], opos[:]),
                       waits=[hcp, hc2] + ([out_dma] if out_dma else []))
            out_dma = p.dma("gpsimd", sout[j], obf[:], ost, waits=[cp1])
            guards["q"] = out_dma; guards["k"] = out_dma; guards["v"] = out_dma; obf_free = out_dma
            continue
        if stage == 'gate':
            cp0 = p.op("vector", lambda h: h.tensor_copy(obf[:, 0:2048], qT[:, 0:2048]),
                       waits=[tq_T, ap0] + ([out_dma] if out_dma else []))
            cp1 = p.op("vector", lambda h: h.tensor_copy(obf[:, 2048:4096], sg_ap[:]))
            out_dma = p.dma("gpsimd", sout[j], obf[:], ost, waits=[cp1])
            guards["q"] = out_dma; guards["k"] = out_dma; guards["v"] = out_dma; obf_free = out_dma
            continue
        if stage == 'rawout':
            cp1 = p.op("vector", lambda h: h.tensor_copy(obf[:, 0:2048], opos[:]),
                       waits=[ap0] + ([out_dma] if out_dma else []))
            out_dma = p.dma("gpsimd", sout[j][:, 0:2048], obf[:, 0:2048], ost, waits=[cp1])
            guards["q"] = out_dma; guards["k"] = out_dma; guards["v"] = out_dma; obf_free = out_dma
            continue
        gb_mm = hb_mm = dh_mm = None
        for t in range(16):
            gb_mm = p.op("tensor", lambda h, t=t: h.matmul(
                PA[:, t * 128:(t + 1) * 128], kT[:, t * 128:(t + 1) * 128], wT_bf[0][:],
                start=True, stop=True), waits=[ap0mm, ap0, tk_T] if t == 0 else (),
                inc=(1 if t == 15 else 0))
        ev_gb = p.op("vector", lambda h: h.tensor_copy(gb_s[:], PA[:]), waits=[gb_mm, ap0])
        for t in range(16):
            hb_mm = p.op("tensor", lambda h, t=t: h.matmul(
                PA[:, t * 128:(t + 1) * 128], kT[:, t * 128:(t + 1) * 128], wT_bf[2][:],
                start=True, stop=True), waits=[ev_gb] if t == 0 else (),
                inc=(1 if t == 15 else 0))
        ev_hb = p.op("vector", lambda h: h.tensor_copy(hb_s[:], PA[:]), waits=[hb_mm])
        for t in range(16):
            dh_mm = p.op("tensor", lambda h, t=t: h.matmul(
                PA[:, t * 128:(t + 1) * 128], vT[:, t * 128:(t + 1) * 128], w_bf[1][:],
                start=True, stop=True), waits=[ev_hb, tv_T] if t == 0 else (),
                inc=(1 if t == 15 else 0))
        ev_dh = p.op("vector", lambda h: h.tensor_copy(dh_s[:], PA[:]), waits=[dh_mm])
        u1 = p.op("scalar", lambda h: h.activation(sgb[:], gb_s[:], AF.Silu), waits=[ev_gb])
        u2 = p.op("scalar", lambda h: h.activation(ssig[:], gb_s[:], AF.Sigmoid))
        u3 = p.op("vector", lambda h: h.tensor_tensor(hid_s[:], sgb[:], hb_s[:], ALU.mult),
                  waits=[u1, ev_hb])
        u4 = p.op("vector", lambda h: h.tensor_tensor(dhb_s[:], dh_s[:], sgb[:], ALU.mult),
                  waits=[ev_dh])
        u5 = p.op("vector", lambda h: h.tensor_tensor(tmp_s[:], dh_s[:], hb_s[:], ALU.mult))
        u6 = p.op("vector", lambda h: h.tensor_scalar(hb_s[:], ssig[:], -1.0, 1.0, ALU.mult, ALU.add),
                  waits=[u2])
        u7 = p.op("vector", lambda h: h.tensor_tensor(hb_s[:], sgb[:], hb_s[:], ALU.mult))
        u8 = p.op("vector", lambda h: h.tensor_tensor(hb_s[:], ssig[:], hb_s[:], ALU.add))
        u9 = p.op("vector", lambda h: h.tensor_tensor(dgb_s[:], tmp_s[:], hb_s[:], ALU.mult))
        u10 = p.op("vector", lambda h, lr1c=lr1c: h.tensor_tensor(
            hl_s[:].rearrange("p (t c) -> p t c", c=128),
            hid_s[:].rearrange("p (t c) -> p t c", c=128),
            lr1c.broadcast_to([128, 16, 128]), ALU.mult), waits=[aux_done])
        u11 = p.op("vector", lambda h, lr0c=lr0c: h.tensor_tensor(
            kl0_s[:].rearrange("p (t c) -> p t c", c=128),
            kpos[:, 0:2048].rearrange("p (t c) -> p t c", c=128),
            lr0c.broadcast_to([128, 16, 128]), ALU.mult))
        u12 = p.op("vector", lambda h, lr2c=lr2c: h.tensor_tensor(
            kl2_s[:].rearrange("p (t c) -> p t c", c=128),
            kpos[:, 0:2048].rearrange("p (t c) -> p t c", c=128),
            lr2c.broadcast_to([128, 16, 128]), ALU.mult))
        dws = []
        for wi, (lh, rh, off) in enumerate(((dgb_s, kl0_s, 0), (vpos, hl_s, 512), (dhb_s, kl2_s, 1024))):
            dmm = None
            for t in range(16):
                dmm = p.op("tensor", lambda h, t=t, lh=lh, rh=rh, off=off: h.matmul(
                    PB[:, off:off + 128], lh[:, t * 128:(t + 1) * 128], rh[:, t * 128:(t + 1) * 128],
                    start=(t == 0), stop=(t == 15)),
                    waits=[u9, u10, u11, u12] if t == 0 else (), inc=(1 if t == 15 else 0))
            dws.append(dmm)
            guards["v"] = dmm if wi == 1 else guards["v"]
        ev0 = p.op("vector", lambda h: h.tensor_copy(dwf[0][:], PB[:, 0:128]), waits=[dws[0]])
        ev1 = p.op("vector", lambda h: h.tensor_copy(dwf[1][:], PB[:, 512:640]), waits=[dws[1]])
        ev2b = p.op("vector", lambda h: h.tensor_copy(dwf[2][:], PB[:, 1024:1152]), waits=[dws[2]])
        dwev = [ev0, ev1, ev2b]
        guards["k"] = u12
        guards["q"] = tq_T
        if stage == 'dw':
            cp1 = p.op("vector", lambda h: h.tensor_copy(obf[:, 0:128], dwf[0][:]),
                       waits=dwev + ([out_dma] if out_dma else []))
            cp2 = p.op("vector", lambda h: h.tensor_copy(obf[:, 128:256], dwf[1][:]))
            cp3 = p.op("vector", lambda h: h.tensor_copy(obf[:, 256:384], dwf[2][:]))
            out_dma = p.dma("gpsimd", sout[j][:, 0:384], obf[:, 0:384], ost, waits=[cp3])
            guards["q"] = out_dma; guards["k"] = out_dma; guards["v"] = out_dma; obf_free = out_dma
            continue
        mi_mm = None
        for t in range(16):
            mi_mm = p.op("tensor", lambda h, t=t, j=j: h.matmul(
                PB[0:1, 1280:1281], aux_v[:, t, 24 + j:25 + j], ns.ones_f32[:],
                start=(t == 0), stop=(t == 15)),
                waits=[ev2b, aux_done] if t == 0 else (), inc=(1 if t == 15 else 0))
        mi_ev = p.op("vector", lambda h: h.tensor_scalar(srow[:, 0:1], PB[0:1, 1280:1281],
                                                         1.0 / 2048, 1.0, ALU.mult, ALU.add),
                     waits=[mi_mm])
        prev_ns_last = None
        for wi in range(3):
            f1 = p.op("scalar", lambda h, wi=wi: h.activation(scr1[:], dwf[wi][:], AF.Square),
                      waits=[dwev[wi]] + ([prev_ns_last] if prev_ns_last else []))
            f2 = p.op("vector", lambda h: h.tensor_reduce(colr[:], scr1[:], AX.X, ALU.add),
                      waits=[f1])
            f3 = p.op("tensor", lambda h: h.matmul(PB[0:1, 1288:1289], colr[:], ns.ones_f32[:],
                                                   start=True, stop=True), waits=[f2, mi_ev], inc=1)
            f4 = p.op("scalar", lambda h: h.activation(srow[:, 1:2], PB[0:1, 1288:1289], AF.Sqrt),
                      waits=[f3])
            f5 = p.op("vector", lambda h: h.tensor_tensor(srow[:, 2:3], srow[:, 0:1], srow[:, 1:2],
                                                          ALU.mult), waits=[f4, mi_ev])
            f6 = p.op("vector", lambda h: h.tensor_scalar_add(srow[:, 2:3], srow[:, 2:3], 1e-7))
            f7 = p.op("vector", lambda h: h.reciprocal(srow[:, 3:4], srow[:, 2:3]))
            f8 = p.op("vector", lambda h: h.tensor_tensor(srow[:, 4:5], srow[:, 0:1], srow[:, 3:4],
                                                          ALU.mult))
            f9 = p.op("tensor", lambda h: h.matmul(PB[:, 1292:1293], ns.ones_row[:], srow[:, 4:5],
                                                   start=True, stop=True), waits=[f8], inc=1)
            f10 = p.op("vector", lambda h: h.tensor_copy(fab[:], PB[:, 1292:1293]), waits=[f9])
            f11 = p.op("vector", lambda h, wi=wi: h.tensor_scalar(dwf[wi][:], dwf[wi][:], fab[:],
                                                                  None, ALU.mult), waits=[f10])
            f12 = p.op("vector", lambda h, wi=wi: h.tensor_copy(xbf[:], dwf[wi][:]))
            last = f12
            for it, (na, nb, ncf) in enumerate(NS_ABC):
                g1 = p.op("tensor", lambda h: h.transpose(PT[:, 0:128], xbf[:], idbf[:]),
                          waits=[last, prev_pt_ev], inc=1)
                g2 = p.op("vector", lambda h: h.tensor_copy(xtb[:], PT[:, 0:128]), waits=[g1])
                prev_pt_ev = g2
                g3 = p.op("tensor", lambda h: h.matmul(PB[:, 128:256], xtb[:], xtb[:],
                                                       start=True, stop=True), waits=[g2], inc=1)
                g4 = p.op("vector", lambda h: h.tensor_copy(a32[:], PB[:, 128:256]), waits=[g3])
                g5 = p.op("vector", lambda h: h.tensor_copy(abf[:], a32[:]))
                g6 = p.op("tensor", lambda h: h.matmul(PB[:, 640:768], abf[:], abf[:],
                                                       start=True, stop=True), waits=[g5], inc=1)
                g7 = p.op("vector", lambda h, nb=nb: h.tensor_scalar_mul(scr1[:], a32[:], float(nb)))
                g8 = p.op("vector", lambda h, ncf=ncf: h.scalar_tensor_tensor(
                    scr1[:], PB[:, 640:768], float(ncf), scr1[:], ALU.mult, ALU.add), waits=[g6])
                g9 = p.op("vector", lambda h: h.tensor_copy(bmbf[:], scr1[:]))
                g10 = p.op("tensor", lambda h: h.matmul(PB[:, 128:256], bmbf[:], xbf[:],
                                                        start=True, stop=True), waits=[g9, g4], inc=1)
                g11 = p.op("vector", lambda h, wi=wi, na=na: h.tensor_scalar_mul(
                    scr1[:], dwf[wi][:], float(na)))
                g12 = p.op("vector", lambda h, wi=wi: h.tensor_tensor(
                    dwf[wi][:], PB[:, 128:256], scr1[:], ALU.add), waits=[g10])
                last = p.op("vector", lambda h, wi=wi: h.tensor_copy(xbf[:], dwf[wi][:]))
            h1 = p.op("vector", lambda h, wi=wi: h.tensor_tensor(w_st[wi][:], w_st[wi][:],
                                                                 dwf[wi][:], ALU.add), waits=[last])
            h2 = p.op("scalar", lambda h, wi=wi: h.activation(scr1[:], w_st[wi][:], AF.Square),
                      waits=[h1])
            h3 = p.op("vector", lambda h: h.tensor_reduce(colr[:], scr1[:], AX.X, ALU.add),
                      waits=[h2])
            h4 = p.op("scalar", lambda h: h.activation(colr[:], colr[:], AF.Sqrt), waits=[h3])
            h5 = p.op("vector", lambda h: h.tensor_scalar_add(colr[:], colr[:], 1e-5), waits=[h4])
            h6 = p.op("vector", lambda h: h.reciprocal(colr[:], colr[:]))
            h7 = p.op("vector", lambda h, wi=wi: h.tensor_tensor(colr[:], w_n0[wi][:], colr[:],
                                                                 ALU.mult))
            h8 = p.op("vector", lambda h, wi=wi: h.tensor_scalar(w_st[wi][:], w_st[wi][:],
                                                                 colr[:], None, ALU.mult))
            h9 = p.op("vector", lambda h, wi=wi: h.tensor_copy(w_bf[wi][:], w_st[wi][:]))
            tt_ = pe_t_batch(w_bf[wi], wT_bf[wi], 1, [h9], f"wT2_{wi}")
            prev_ns_last = tt_
        if stage == 'wnew':
            cp1 = p.op("vector", lambda h: h.tensor_copy(obf[:, 0:128], w_st[0][:]),
                       waits=[prev_ns_last, h8] + ([out_dma] if out_dma else []))
            cp2 = p.op("vector", lambda h: h.tensor_copy(obf[:, 128:256], w_st[1][:]))
            cp3 = p.op("vector", lambda h: h.tensor_copy(obf[:, 256:384], w_st[2][:]))
            out_dma = p.dma("gpsimd", sout[j][:, 0:384], obf[:, 0:384], ost, waits=[cp3])
            guards["q"] = out_dma; guards["k"] = out_dma; guards["v"] = out_dma; obf_free = out_dma
            continue
        ap1, ap1mm, ap1ev = apply_chunk(1, [prev_ns_last, prev_pt_ev])
        guards["w"] = ap1mm
        guards["w2"] = ap1
        if stage == 'obf2':
            cpx = p.op("vector", lambda h: h.tensor_copy(oevb[:], obf[:, 0:1024]),
                       waits=[ap1, ap0] + ([out_dma] if out_dma else []))
            out_dma = p.dma("gpsimd", sout[j], obf[:], ost, waits=[cpx])
            guards["q"] = out_dma; guards["k"] = out_dma; guards["v"] = out_dma; obf_free = out_dma
            continue
        for grp in range(4):
            last_t = None
            for u in range(8):
                ti = grp * 8 + u
                last_t = p.op("tensor", lambda h, ti=ti, u=u: h.transpose(
                    PT[:, u * 128:(u + 1) * 128], obf[:, ti * 128:(ti + 1) * 128], idbf[:]),
                    waits=([ap1, ap0, prev_pt_ev] if u == 0 else ()), inc=(1 if u == 7 else 0))
            prev_pt_ev = p.op("vector", lambda h, grp=grp: h.tensor_copy(oevb[:], PT[:]),
                              waits=[last_t] + ([out_dma] if out_dma else []))
            out_dma = p.dma("gpsimd", sout[j][:, grp * 1024:(grp + 1) * 1024], oevb[:], ost,
                            waits=[prev_pt_ev])
        obf_free = out_dma
    p.wait("gpsimd", [("ost", p.sem_val["ost"])])
    return p.build()


def run_l1_attn(inp, w, x_fm, lnx, _run):
    nc_e1 = _cached("l1attn", build_l1_attn_core)
    auxw, auxbias = [], []
    W_lr = np.asarray(w.aux_b[:, 0:48])
    W_m = np.asarray(w.aux_b[:, 48:64])
    W_s = np.asarray(w.aux_b[:, 64:80])
    b_m = np.asarray(inp["L1_attn_b_m"], np.float32)
    b_s = np.asarray(inp["L1_attn_b_s"], np.float32)
    lng = np.concatenate([np.asarray(inp["L1_attn_ln_g"], np.float32),
                          np.asarray(inp["L1_attn_ln_b"], np.float32)])[None, :]
    in_maps = []
    for c in range(NC):
        b, half = c // 2, c % 2
        hs = slice(8 * half, 8 * half + 8)
        waux_c = np.ascontiguousarray(np.concatenate(
            [W_lr[:, 0 + 8 * half:0 + 8 * half + 8], W_lr[:, 16 + 8 * half:16 + 8 * half + 8],
             W_lr[:, 32 + 8 * half:32 + 8 * half + 8], W_m[:, hs], W_s[:, hs]], axis=1))
        auxb_c = np.ascontiguousarray(np.concatenate(
            [np.full(24, BASE_LR_INV, np.float32), b_m[hs], b_s[hs]])[None, :])
        lnx_full = np.concatenate([lnx[2 * b], lnx[2 * b + 1]], axis=2)
        in_maps.append({
            "lnx_own": lnx[c], "lnx_full": lnx_full, "wqkv": w.qkv_b,
            "waux": waux_c, "auxb": auxb_c, "lngb": lng,
            "w0": np.ascontiguousarray(np.asarray(inp["L1_attn_w0"], np.float32)[hs]),
            "w1": np.ascontiguousarray(np.asarray(inp["L1_attn_w1"], np.float32)[hs]),
            "w2": np.ascontiguousarray(np.asarray(inp["L1_attn_w2"], np.float32)[hs]),
        })
    res = _run(nc_e1, in_maps, core_ids=list(range(NC)))
    souts = [res.results[c]["sout"] for c in range(NC)]

    nc_e2 = _cached("projres", lambda: build_proj_residual(scale=1.0))
    zb = np.zeros((128, 16), np.float32)
    in_maps = []
    for c in range(NC):
        b, half = c // 2, c % 2
        tsl = slice(half * TPC, (half + 1) * TPC)
        hin = np.stack([souts[2 * b + hc // 8][hc % 8][:, tsl] for hc in range(16)], 0)
        in_maps.append({"hin": hin, "w": w.wo1_b, "b": zb, "x": x_fm[c]})
    res = _run(nc_e2, in_maps, core_ids=list(range(NC)))
    return [res.results[c]["xo"] for c in range(NC)], [res.results[c]["lnxo"] for c in range(NC)]
